# version 02
# baseline (speedup 1.0000x reference)
import numpy as np
import ml_dtypes
from contextlib import ExitStack

import concourse.bass as bass
import concourse.mybir as mybir
from concourse.bass_utils import run_bass_kernel_spmd

F32 = mybir.dt.float32
BF16 = mybir.dt.bfloat16
AF = mybir.ActivationFunctionType
ALU = mybir.AluOpType
AXL = mybir.AxisListType
NPBF = ml_dtypes.bfloat16

B, S, D = 2, 16384, 2048
DI, DS, NH, HD = 4096, 128, 64, 64
CONV = 4352
PROJ = 8512
CH = 128
EPS = 1e-5
NC = 8
TOK = B * S // NC
TILE = 512
NCH = TILE // CH
NE, NSH, FF = 16, 2, 1024
MASK_NEG = -30000.0
WINC = 256
NWIN = PROJ // WINC + (1 if PROJ % WINC else 0)
NT_BUILD = 8

ENG2BLK = {"pe": "tensor", "dve": "vector", "act": "scalar", "pool": "gpsimd",
           "sp": "sync"}


class Buf:
    __slots__ = ("w", "rs", "name")

    def __init__(self, name):
        self.name, self.w, self.rs = name, None, []


class Prog:
    def __init__(self, nc, ctx, block):
        self.nc, self.ctx, self.block = nc, ctx, block
        self.es = {}
        for e in ENG2BLK:
            self.es[e] = [ctx.enter_context(nc.semaphore(f"es_{e}")), 0]
        self.dma_sems = {}
        self.waited = {e: {} for e in ENG2BLK}
        self.nstages = 0
        self.bufs = {}

    def gb(self, name):
        if name not in self.bufs:
            self.bufs[name] = Buf(name)
        return self.bufs[name]

    def dma_sem(self, name):
        if name not in self.dma_sems:
            self.dma_sems[name] = [
                self.ctx.enter_context(self.nc.semaphore(f"dm_{name}")), 0]
        return self.dma_sems[name]

    def _waits(self, E, en, deps):
        best = {}
        for (h, v) in deps:
            k = id(h)
            if v > best.get(k, (0, None))[0]:
                best[k] = (v, h)
        for k, (v, h) in best.items():
            if self.waited[en].get(k, -1) < v:
                E.wait_ge(h, v)
                self.waited[en][k] = v

    def _deps(self, reads, writes, deps):
        dl = list(deps)
        for b in reads:
            if b.w is not None:
                dl.append(b.w)
        for b in writes:
            if b.w is not None:
                dl.append(b.w)
            dl.extend(b.rs)
        return dl

    def _book(self, tok, reads, writes):
        for b in reads:
            b.rs.append(tok)
        for b in writes:
            b.w, b.rs = tok, []

    def stage(self, en, body, reads=(), writes=(), deps=()):
        dl = self._deps(reads, writes, deps)
        sem = self.es[en]
        dec = getattr(self.block, ENG2BLK[en])

        @dec
        def _(E):
            self._waits(E, en, dl)
            last = body(E)
            assert last is not None
            last.then_inc(sem[0], 1)
        sem[1] += 1
        tok = (sem[0], sem[1])
        self._book(tok, reads, writes)
        self.nstages += 1
        return tok

    def barrier(self, extra=()):
        toks = [(s[0], s[1]) for s in self.es.values()]
        toks += [(s[0], s[1]) for s in self.dma_sems.values()]
        toks += list(extra)
        for en in ENG2BLK:
            dec = getattr(self.block, ENG2BLK[en])

            @dec
            def _(E, en=en):
                self._waits(E, en, toks)
        for b in self.bufs.values():
            b.w, b.rs = None, []

    def dma(self, qn, slot, pairs, reads=(), writes=(), deps=()):
        dl = self._deps(reads, writes, deps)
        sem = self.dma_sem(slot)
        dec = getattr(self.block, ENG2BLK[qn])

        @dec
        def _(E):
            self._waits(E, qn, dl)
            for pr in pairs:
                kw = pr[2] if len(pr) > 2 else {}
                E.dma_start(out=pr[0], in_=pr[1], **kw).then_inc(sem[0], 16)
        sem[1] += 16 * len(pairs)
        tok = (sem[0], sem[1])
        self._book(tok, reads, writes)
        return tok


class Arena:

    def __init__(self, master, total_f32):
        self.m = master
        self.total = total_f32
        self.off = 0
        self.marks = []

    def push(self):
        self.marks.append(self.off)

    def pop(self):
        self.off = self.marks.pop()

    def f32(self, n, parts=128):
        o = self.off
        self.off += n
        assert self.off <= self.total, f"arena overflow {self.off}>{self.total}"
        return self.m[0:parts, o:o + n]

    def bf16(self, n, parts=128):
        o = self.off
        self.off += (n + 1) // 2
        assert self.off <= self.total, f"arena overflow {self.off}>{self.total}"
        return self.m[0:parts, o:o + (n + 1) // 2].bitcast(BF16)


def build_bass(n_tiles=8, n_cores=NC):
    NT = n_tiles
    nc = bass.Bass("TRN2", target_bir_lowering=False)

    def din(name, shape, dt=F32):
        return nc.dram_tensor(name, shape, dt, kind="ExternalInput").ap()

    x_in = din("x", [TOK, D])
    xhalo = din("xhalo", [4, D])
    w_in = din("w_in", [D, PROJ])
    w_out = din("w_out", [DI, D])
    conv_w_rs = din("conv_w_rs", [128, 34 * 4])
    conv_b_rs = din("conv_b_rs", [128, 34])
    a_col = din("a_col", [64, 1])
    dtb_col = din("dtb_col", [64, 1])
    d_bc = din("d_bc", [128, 64])
    norm_w_rs = din("norm_w_rs", [128, 32])
    wa_rs = din("wa_rs", [128, 256])
    rg_in = din("rg", [NE, D, FF])
    rw1_in = din("rw1", [NE, D, FF])
    rw2_in = din("rw2", [NE, FF, D])
    sg_in = din("sg", [NSH, D, FF])
    sw1_in = din("sw1", [NSH, D, FF])
    sw2_in = din("sw2", [NSH, FF, D])
    rgb_rs = din("rgb_rs", [128, NE * 8])
    rb1_rs = din("rb1_rs", [128, NE * 8])
    sgb_rs = din("sgb_rs", [128, NSH * 8])
    sb1_rs = din("sb1_rs", [128, NSH * 8])
    rb2_in = din("rb2_rs", [1, NE * 16 * 128])
    sb2_in = din("sb2_rs", [1, NSH * 16 * 128])
    eye_bf_in = din("eye_bf", [128, 128], BF16)
    eye_f_in = din("eye_f", [128, 128])
    m_up_in = din("m_up", [128, 128])
    ones_in = din("ones_in", [128, 8])
    onesr_in = din("onesr_in", [1, 512])
    cst_in = din("cst", [128, 8])
    selT_in = din("selT", [8, 8])
    maskT_in = din("maskT", [64, 8])
    out_t = nc.dram_tensor("out", [TOK, D], F32, kind="ExternalOutput").ap()

    wbf_gw1 = nc.dram_tensor("wbf_gw1", [NE + NSH, 2, 16, 128, 1024],
                             BF16).ap()
    wbf_w2 = nc.dram_tensor("wbf_w2", [NE + NSH, 8, 128, D], BF16).ap()
    wbf_out = nc.dram_tensor("wbf_out", [32, 128, D], BF16).ap()
    st_xn = nc.dram_tensor("st_xn", [NT, 16, 128, TILE], BF16).ap()
    st_sz = nc.dram_tensor("st_sz", [NT, 32, 128, TILE], BF16).ap()
    st_yl = nc.dram_tensor("st_yl", [NT, NCH, 128, DI], BF16).ap()
    st_c = nc.dram_tensor("st_c", [NT, NCH, 128, 128], BF16).ap()
    st_dacs = nc.dram_tensor("st_dacs", [NT, 64, TILE], F32).ap()
    st_dalast = nc.dram_tensor("st_dalast", [NT, NCH, 64], F32).ap()
    st_cmb = nc.dram_tensor("st_cmb", [NT, NE, TILE], F32).ap()
    st_s = nc.dram_tensor("st_s", [NT, 16, 128, TILE], BF16).ap()
    cc_in = nc.dram_tensor("cc_in", [128, 4160], BF16).ap()
    cc_out = nc.dram_tensor("cc_out", [n_cores * 128, 4160], BF16,
                            addr_space="Shared" if n_cores > 4
                            else "Local").ap()

    with ExitStack() as ctx:
        MASTER_N = 48896
        master = ctx.enter_context(
            nc.sbuf_tensor("master", [128, MASTER_N], F32))[:, :]

        def sbt(name, shape, dt):
            return ctx.enter_context(nc.sbuf_tensor(name, shape, dt))

        eye_bf = sbt("eye_bf_s", [128, 128], BF16)
        eye_f = sbt("eye_f_s", [128, 128], F32)
        m_up = sbt("m_up_s", [128, 128], F32)
        ones_c = sbt("ones_c", [128, 8], F32)
        ones_r = sbt("ones_r", [1, 512], F32)
        conv_w_s = sbt("conv_w_s", [128, 136], F32)
        conv_b_s = sbt("conv_b_s", [128, 34], F32)
        a_s = sbt("a_s", [64, 1], F32)
        dtb_s = sbt("dtb_s", [64, 1], F32)
        dbc_s = sbt("dbc_s", [128, 64], F32)
        nw_s = sbt("nw_s", [128, 32], F32)
        wa_f = sbt("wa_f", [128, 256], F32)
        wa_hi = sbt("wa_hi", [128, 256], BF16)
        wa_lo = sbt("wa_lo", [128, 256], BF16)
        rgb_s = sbt("rgb_s", [128, NE * 8], F32)
        rb1_s = sbt("rb1_s", [128, NE * 8], F32)
        sgb_s = sbt("sgb_s", [128, NSH * 8], F32)
        sb1_s = sbt("sb1_s", [128, NSH * 8], F32)
        selT_s = sbt("selT_s", [8, 8], F32)
        maskT_s = sbt("maskT_s", [64, 8], F32)
        cst_s = sbt("cst_s", [128, 8], F32)
        ln_cols = sbt("ln_cols", [128, 8], F32)
        smallrow = sbt("smallrow", [1, 128], F32)
        tail = sbt("tail", [128, 34 * 3], BF16)
        prevs_t = sbt("prevs_t", [128, DI], BF16)
        ld_col = sbt("ld_col", [64, 1], F32)
        zrow = sbt("zrow", [128, 64], BF16)
        halo_fm = sbt("halo_fm", [128, 16 * 4], BF16)

        psum = [ctx.enter_context(nc.psum_tensor(f"ps{i}", [128, 512], F32))
                for i in range(8)]

        block = ctx.enter_context(nc.Block())
        p = Prog(nc, ctx, block)
        gb = p.gb
        pbank = [gb(f"ps{i}") for i in range(8)]
        ar = Arena(master, MASTER_N)

        ar.push()
        x_tm = [ar.f32(D) for _ in range(2)]
        sq_scr = ar.f32(D)
        xn_tm_hi = ar.bf16(D)
        xn_tm_lo = ar.bf16(D)
        xn_fm_hi = ar.bf16(16 * TILE)
        xn_fm_lo = ar.bf16(16 * TILE)
        w_win = [ar.bf16(16 * WINC) for _ in range(2)]
        sz_ring = [ar.bf16(512) for _ in range(4)]
        xbcp = [ar.bf16(520) for _ in range(4)]
        conv_scr = [ar.f32(512) for _ in range(2)]
        x_fm = ar.bf16(32 * 512)
        b_fm = ar.bf16(512)
        c_fm = ar.bf16(512)
        dt_fm = ar.f32(512, parts=64)
        dt_scr = ar.f32(512, parts=64)
        dacs_fm = ar.f32(512, parts=64)
        zeros64 = ar.f32(128, parts=64)
        darg_fm = ar.f32(128, parts=64)
        dacs_tm = ar.f32(64)
        dt_tm = ar.f32(64)
        dec_tm = ar.f32(64)
        cdec_bc = ar.f32(64)
        b_tm = ar.bf16(128)
        bc_sb = ar.f32(128)
        rowbuf = [ar.f32(1024, parts=1) for _ in range(2)]
        x_tm_c = ar.bf16(DI)
        dtx = ar.bf16(DI)
        bce_sb = [ar.f32(512) for _ in range(2)]
        csc_sb = [ar.bf16(512) for _ in range(2)]
        lt_sb = [ar.f32(128) for _ in range(2)]
        stx_sb = [ar.bf16(128) for _ in range(2)]
        scan_scr = ar.f32(512)
        ylc = [ar.bf16(DI)]
        ylc = [ylc[0], ylc[0]]
        aff_tm = ar.f32(16)
        cmb_scr = ar.f32(48)
        cmb_fm = ar.f32(512, parts=16)
        tcw = [ar.bf16(1024) for _ in range(2)]
        tco_f = ar.f32(512)
        tco_b = ar.bf16(512)
        a_top = ar.off
        ar.pop()
        print(f"[arena] phase A: {a_top * 4 / 1024:.1f} KiB/part")

        ar.push()
        gthr = [ar.bf16(DI) for _ in range(2)]
        fold_scr = ar.f32(DI)
        ex_scr = ar.f32(512)
        ldg = ar.f32(64, parts=8)
        warg = ar.f32(8, parts=64)
        yl_b = [ar.bf16(DI) for _ in range(2)]
        sz_b = ar.bf16(32 * 512)
        yn_b = ar.bf16(32 * 512)
        yg_sb = ar.f32(DI)
        ygsq = ar.f32(128)
        rstd_r = ar.f32(128, parts=1)
        wo_win = [ar.bf16(32 * 256) for _ in range(2)]
        s_ev = [ar.bf16(512) for _ in range(4)]
        c_b = [ar.bf16(128) for _ in range(2)]
        rowb1 = [ar.f32(1024, parts=1) for _ in range(2)]
        srow64 = ar.f32(64, parts=1)
        cdec_b1 = ar.f32(64)
        bce_b1 = [ar.f32(512) for _ in range(2)]
        csc_b1 = [ar.bf16(512) for _ in range(2)]
        b1_top = ar.off
        ar.pop()
        print(f"[arena] phase B1: {b1_top * 4 / 1024:.1f} KiB/part")

        ar.push()
        xn_b2 = ar.bf16(16 * TILE)
        ffn_acc = ar.f32(16 * TILE)
        h_b2 = ar.bf16(8 * TILE)
        w2full = ar.bf16(8 * D)
        wgw1 = [ar.bf16(1024) for _ in range(4)]
        s_ld = [ar.bf16(16 * 128) for _ in range(2)]
        s_tm = ar.f32(D)
        g_tm = ar.f32(D)
        x_ld = [ar.f32(D) for _ in range(2)]
        out_b = [ar.f32(D) for _ in range(2)]
        gw_scr = ar.f32(512)
        w1_scr = ar.f32(512)
        cmb_row = ar.f32(512, parts=1)
        cmb_bc = ar.f32(512)
        rb2_row = ar.f32(512, parts=1)
        b2_top = ar.off
        ar.pop()
        print(f"[arena] phase B2: {b2_top * 4 / 1024:.1f} KiB/part")
        inc_bf = prevs_t[:, :]

        p.dma("sp", "pre", [
            (eye_bf[:, :], eye_bf_in), (eye_f[:, :], eye_f_in),
            (m_up[:, :], m_up_in), (ones_c[:, :], ones_in),
            (ones_r[:1, :], onesr_in), (conv_w_s[:, :], conv_w_rs),
            (conv_b_s[:, :], conv_b_rs), (a_s[:, :], a_col),
            (dtb_s[:, :], dtb_col), (dbc_s[:, :], d_bc),
            (nw_s[:, :], norm_w_rs), (wa_f[:, :], wa_rs),
            (rgb_s[:, :], rgb_rs), (rb1_s[:, :], rb1_rs),
            (sgb_s[:, :], sgb_rs), (sb1_s[:, :], sb1_rs),
            (selT_s[:, :], selT_in), (maskT_s[:, :], maskT_in),
            (cst_s[:, :], cst_in),
        ], writes=[gb("globals")])
        p.stage("dve", lambda E: E.tensor_copy(out=wa_hi[:, :], in_=wa_f[:, :]),
                reads=[gb("globals")], writes=[gb("wa_hi")])
        p.stage("dve", lambda E: E.tensor_tensor(
            out=wa_lo[:, :], in0=wa_f[:, :], in1=wa_hi[:, :],
            op=ALU.subtract),
            reads=[gb("globals"), gb("wa_hi")], writes=[gb("wa_lo")])
        p.stage("pool", lambda E: (
            E.memset(prevs_t[:, :], 0.0), E.memset(ld_col[:, :], 0.0),
            E.memset(zeros64, 0.0), E.memset(tail[:, :], 0.0),
            E.memset(zrow[:, :], 0.0))[-1],
            writes=[gb("prevs"), gb("ld"), gb("z64"), gb("tail"),
                    gb("ldbf")])

        tc_jobs = [("wo", kc, cc4) for kc in range(32) for cc4 in range(4)]
        for e in range(NE + NSH):
            ei = e if e < NE else e - NE
            gs = rg_in if e < NE else sg_in
            w1s = rw1_in if e < NE else sw1_in
            w2s = rw2_in if e < NE else sw2_in
            for half in range(2):
                for kc in range(16):
                    tc_jobs.append(("gw1", e, half, kc, gs, w1s, ei))
            for kc in range(8):
                for half in range(2):
                    tc_jobs.append(("w2", e, kc, half, w2s, ei))
        tc_i = [0]

        def emit_transcode(n):
            for _ in range(n):
                if tc_i[0] >= len(tc_jobs):
                    return
                job = tc_jobs[tc_i[0]]
                tc_i[0] += 1
                sl = tc_i[0] % 2
                bb = gb(f"tcw{sl}")
                if job[0] == "wo":
                    _, kc, cc4 = job
                    p.dma("pool", "tcof",
                          [(tco_f, w_out[kc * 128:(kc + 1) * 128,
                                         cc4 * 512:(cc4 + 1) * 512])],
                          writes=[gb("tco_f")])
                    p.stage("dve", lambda E, kc=kc: E.tensor_scalar(
                        out=tco_b, in0=tco_f, scalar1=nw_s[:, kc:kc + 1],
                        scalar2=None, op0=ALU.mult),
                        reads=[gb("tco_f"), gb("globals")],
                        writes=[gb("tco_b")])
                    p.dma("pool", "tcob",
                          [(wbf_out[kc, :, cc4 * 512:(cc4 + 1) * 512], tco_b)],
                          reads=[gb("tco_b")], writes=[gb("D_wo")])
                elif job[0] == "gw1":
                    _, e, half, kc, gs, w1s, ei = job
                    p.dma("pool", f"tc{sl}",
                          [(tcw[sl][:, 0:512],
                            gs[ei, kc * 128:(kc + 1) * 128,
                               half * 512:(half + 1) * 512]),
                           (tcw[sl][:, 512:1024],
                            w1s[ei, kc * 128:(kc + 1) * 128,
                                half * 512:(half + 1) * 512])],
                          writes=[bb])
                    p.dma("pool", f"tc{sl}",
                          [(wbf_gw1[e, half, kc, :, :], tcw[sl][:, :])],
                          reads=[bb], writes=[gb("D_gw1")])
                else:
                    _, e, kc, half, w2s, ei = job
                    p.dma("pool", f"tc{sl}",
                          [(tcw[sl][:, :],
                            w2s[ei, kc * 128:(kc + 1) * 128,
                                half * 1024:(half + 1) * 1024])],
                          writes=[bb])
                    p.dma("pool", f"tc{sl}",
                          [(wbf_w2[e, kc, :, half * 1024:(half + 1) * 1024],
                            tcw[sl][:, :])],
                          reads=[bb], writes=[gb("D_w2")])

        def ln_tokens(xb, xbufr, r, sl_hi, sl_lo):
            lc = gb("ln_cols")
            p.stage("dve", lambda E: E.tensor_reduce(
                out=ln_cols[:r, 0:1], in_=xb[:r, :], axis=AXL.X, op=ALU.add),
                reads=[xbufr], writes=[lc])
            p.stage("act", lambda E: E.activation(
                out=sq_scr[:r, :], in_=xb[:r, :], func=AF.Square,
                accum_out=ln_cols[:r, 1:2]),
                reads=[xbufr], writes=[gb("sq_scr"), lc])
            p.stage("dve", lambda E: E.tensor_scalar(
                out=ln_cols[:r, 2:3], in0=ln_cols[:r, 0:1],
                scalar1=cst_s[:r, 4:5], scalar2=None, op0=ALU.mult),
                reads=[lc, gb("globals")], writes=[lc])
            p.stage("dve", lambda E: E.tensor_tensor(
                out=ln_cols[:r, 3:4], in0=ln_cols[:r, 2:3],
                in1=ln_cols[:r, 2:3], op=ALU.mult),
                reads=[lc], writes=[lc])
            p.stage("dve", lambda E: E.scalar_tensor_tensor(
                out=ln_cols[:r, 3:4], in0=ln_cols[:r, 1:2],
                scalar=cst_s[:r, 4:5],
                in1=ln_cols[:r, 3:4], op0=ALU.mult, op1=ALU.subtract),
                reads=[lc, gb("globals")], writes=[lc])
            p.stage("act", lambda E: E.activation(
                out=ln_cols[:r, 4:5], in_=ln_cols[:r, 3:4], func=AF.Ln,
                bias=cst_s[:r, 0:1]),
                reads=[lc, gb("globals")], writes=[lc])
            p.stage("act", lambda E: E.activation(
                out=ln_cols[:r, 4:5], in_=ln_cols[:r, 4:5], func=AF.Exp,
                scale=cst_s[:r, 3:4]),
                reads=[lc, gb("globals")], writes=[lc])
            p.stage("dve", lambda E: E.tensor_tensor(
                out=ln_cols[:r, 5:6], in0=ln_cols[:r, 2:3],
                in1=ln_cols[:r, 4:5], op=ALU.mult),
                reads=[lc], writes=[lc])
            p.stage("dve", lambda E: E.tensor_scalar(
                out=sq_scr[:r, :], in0=xb[:r, :], scalar1=ln_cols[:r, 4:5],
                scalar2=ln_cols[:r, 5:6], op0=ALU.mult, op1=ALU.subtract),
                reads=[xbufr, lc], writes=[gb("sq_scr")])
            p.stage("dve", lambda E: E.tensor_copy(out=sl_hi, in_=sq_scr[:r, :]),
                    reads=[gb("sq_scr")], writes=[gb("xn_hl")])
            p.stage("pool", lambda E: E.tensor_tensor(
                out=sl_lo, in0=sq_scr[:r, :], in1=sl_hi, op=ALU.subtract),
                reads=[gb("sq_scr"), gb("xn_hl")], writes=[gb("xn_hl")])

        p.dma("sp", "xh", [(x_tm[0][:4, :], xhalo)], writes=[gb("x_tm0")])
        ln_tokens(x_tm[0], gb("x_tm0"), 4, xn_tm_hi[:4, :], xn_tm_lo[:4, :])
        for kc in range(16):
            p.stage("pe", lambda E, kc=kc: E.transpose(
                psum[6][:, :].bitcast(BF16)[:, 0:4],
                xn_tm_hi[:4, kc * 128:(kc + 1) * 128], eye_bf[:4, :4]),
                reads=[gb("xn_hl"), gb("globals")], writes=[pbank[6]])
            p.stage("dve", lambda E, kc=kc: E.tensor_copy(
                out=halo_fm[:, kc * 4:(kc + 1) * 4],
                in_=psum[6][:, :].bitcast(BF16)[:, 0:4]),
                reads=[pbank[6]], writes=[gb("halo_fm")])
        first_conv_w = DI // WINC
        for w in range(first_conv_w, NWIN):
            wcols = min(WINC, PROJ - w * WINC)
            sl = w % 2
            wb = gb(f"w_win{sl}")
            p.dma("pool", f"ww{sl}",
                  [(w_win[sl][:, kc * WINC:kc * WINC + wcols],
                    w_in[kc * 128:(kc + 1) * 128,
                         w * WINC:w * WINC + wcols]) for kc in range(16)],
                  writes=[wb])
            for mc in range((wcols + 127) // 128):
                mcw = min(128, wcols - mc * 128)
                gf = w * WINC + mc * 128
                if gf >= DI + CONV or gf < DI:
                    continue
                bi = 4 + mc % 2

                def hbody(E, sl=sl, mc=mc, mcw=mcw, bi=bi):
                    last = None
                    for kc in range(16):
                        last = E.matmul(
                            psum[bi][:mcw, 0:4],
                            w_win[sl][:, kc * WINC + mc * 128:
                                      kc * WINC + mc * 128 + mcw],
                            halo_fm[:, kc * 4:(kc + 1) * 4],
                            start=(kc == 0), stop=(kc == 15))
                    return last
                p.stage("pe", hbody, reads=[wb, gb("halo_fm")],
                        writes=[pbank[bi]])
                ci = (gf - DI) // 128
                p.stage("dve", lambda E, ci=ci, mcw=mcw, bi=bi: E.tensor_copy(
                    out=tail[:mcw, ci * 3:ci * 3 + 3],
                    in_=psum[bi][:mcw, 0:3]),
                    reads=[pbank[bi]], writes=[gb("tail")])
        emit_transcode(10)

        for t in range(NT):
            for tc in range(NCH):
                sl = tc % 2
                xb = gb(f"x_tm{sl}")
                p.dma("sp", f"xa{sl}",
                      [(x_tm[sl], x_in[t * TILE + tc * 128:
                                       t * TILE + tc * 128 + 128, :])],
                      writes=[xb])
                ln_tokens(x_tm[sl], xb, 128, xn_tm_hi[:, :], xn_tm_lo[:, :])
                for half, (src, dst, tb) in enumerate(
                        [(xn_tm_hi, xn_fm_hi, gb("xn_fm_hi")),
                         (xn_tm_lo, xn_fm_lo, gb("xn_fm_lo"))]):
                    for g8 in range(2):
                        bi = g8 % 2

                        def tb_body(E, src=src, g8=g8, bi=bi):
                            last = None
                            for j in range(8):
                                kc = g8 * 8 + j
                                last = E.transpose(
                                    psum[bi][:, :].bitcast(BF16)[
                                        :, j * 128:(j + 1) * 128],
                                    src[:, kc * 128:(kc + 1) * 128],
                                    eye_bf[:, :])
                            return last
                        p.stage("pe", tb_body,
                                reads=[gb("xn_hl"), gb("globals")],
                                writes=[pbank[bi]])

                        def tb_ev(E, dst=dst, g8=g8, tc=tc, bi=bi):
                            d3 = dst.rearrange("p (k c) -> p k c", c=TILE)[
                                :, g8 * 8:(g8 + 1) * 8, tc * 128:tc * 128 + 128]
                            return E.tensor_copy(
                                out=d3,
                                in_=psum[bi][:, :].bitcast(BF16)[:, 0:1024])
                        p.stage("dve", tb_ev, reads=[pbank[bi]], writes=[tb])
            def rbody(E):
                last = None
                n = 0
                for kc in range(16):
                    for (lh, rh) in ((wa_hi, xn_fm_hi), (wa_lo, xn_fm_hi),
                                     (wa_hi, xn_fm_lo)):
                        n += 1
                        last = E.matmul(
                            psum[7][:16, 0:512], lh[:, kc * 16:(kc + 1) * 16],
                            rh[:, kc * TILE:(kc + 1) * TILE],
                            start=(n == 1), stop=(n == 48))
                return last
            p.stage("pe", rbody,
                    reads=[gb("wa_hi"), gb("wa_lo"), gb("xn_fm_hi"),
                           gb("xn_fm_lo")],
                    writes=[pbank[7]])
            p.stage("act", lambda E: E.activation(
                out=cmb_fm, in_=psum[7][:16, 0:512], func=AF.Sigmoid),
                reads=[pbank[7]], writes=[gb("cmb_fm")])
            for tc in range(NCH):
                p.stage("pe", lambda E, tc=tc: E.transpose(
                    psum[6][:, 0:16],
                    cmb_fm[:, tc * 128:(tc + 1) * 128], eye_f[:16, :16]),
                    reads=[gb("cmb_fm"), gb("globals")], writes=[pbank[6]])
                p.stage("dve", lambda E: E.tensor_copy(
                    out=aff_tm, in_=psum[6][:, 0:16]),
                    reads=[pbank[6]], writes=[gb("aff_tm")])
                cs = cmb_scr
                p.stage("dve", lambda E: E.tensor_reduce(
                    out=cs[:, 0:1], in_=aff_tm, axis=AXL.X, op=ALU.max),
                    reads=[gb("aff_tm")], writes=[gb("cmb_scr")])
                p.stage("dve", lambda E: E.tensor_scalar(
                    out=cs[:, 16:32], in0=aff_tm, scalar1=cs[:, 0:1],
                    scalar2=None, op0=ALU.is_ge),
                    reads=[gb("aff_tm"), gb("cmb_scr")],
                    writes=[gb("cmb_scr")])
                p.stage("dve", lambda E: E.scalar_tensor_tensor(
                    out=cs[:, 32:48], in0=cs[:, 16:32],
                    scalar=cst_s[:, 6:7],
                    in1=aff_tm, op0=ALU.mult, op1=ALU.add),
                    reads=[gb("aff_tm"), gb("cmb_scr"), gb("globals")],
                    writes=[gb("cmb_scr")])
                p.stage("dve", lambda E: E.tensor_reduce(
                    out=cs[:, 1:2], in_=cs[:, 32:48], axis=AXL.X, op=ALU.max),
                    reads=[gb("cmb_scr")], writes=[gb("cmb_scr")])
                p.stage("dve", lambda E: E.tensor_tensor(
                    out=cs[:, 2:3], in0=cs[:, 0:1], in1=cs[:, 1:2],
                    op=ALU.add),
                    reads=[gb("cmb_scr")], writes=[gb("cmb_scr")])
                p.stage("dve", lambda E: E.reciprocal(
                    out=cs[:, 3:4], in_=cs[:, 2:3]),
                    reads=[gb("cmb_scr")], writes=[gb("cmb_scr")])
                p.stage("dve", lambda E: E.tensor_scalar(
                    out=cs[:, 16:32], in0=aff_tm, scalar1=cs[:, 1:2],
                    scalar2=None, op0=ALU.is_ge),
                    reads=[gb("aff_tm"), gb("cmb_scr")],
                    writes=[gb("cmb_scr")])
                p.stage("dve", lambda E: E.tensor_tensor(
                    out=cs[:, 32:48], in0=aff_tm, in1=cs[:, 16:32],
                    op=ALU.mult),
                    reads=[gb("aff_tm"), gb("cmb_scr")],
                    writes=[gb("cmb_scr")])
                p.stage("dve", lambda E: E.tensor_scalar(
                    out=cs[:, 32:48], in0=cs[:, 32:48], scalar1=cs[:, 3:4],
                    scalar2=None, op0=ALU.mult),
                    reads=[gb("cmb_scr")], writes=[gb("cmb_scr")])
                p.stage("pe", lambda E, tc=tc: E.transpose(
                    psum[6][:16, 0:128], cs[:, 32:48], eye_f[:, :]),
                    reads=[gb("cmb_scr"), gb("globals")], writes=[pbank[6]])
                p.stage("dve", lambda E, tc=tc: E.tensor_copy(
                    out=cmb_fm[:, tc * 128:(tc + 1) * 128],
                    in_=psum[6][:16, 0:128]),
                    reads=[pbank[6]], writes=[gb("cmb_fm")])
            p.dma("act", "cmbst", [(st_cmb[t, :, :], cmb_fm)],
                  reads=[gb("cmb_fm")], writes=[gb("D_cmb")])

            xbcp_i = [0]
            for w in range(NWIN):
                wcols = min(WINC, PROJ - w * WINC)
                sl = w % 2
                wb = gb(f"w_win{sl}")
                p.dma("pool", f"ww{sl}",
                      [(w_win[sl][:, kc * WINC:kc * WINC + wcols],
                        w_in[kc * 128:(kc + 1) * 128,
                             w * WINC:w * WINC + wcols]) for kc in range(16)],
                      writes=[wb])
                for mc in range((wcols + 127) // 128):
                    mcw = min(128, wcols - mc * 128)
                    gf = w * WINC + mc * 128
                    bi = (w % 4)

                    def wbody(E, sl=sl, mc=mc, mcw=mcw, bi=bi):
                        last = None
                        for kc in range(16):
                            last = E.matmul(
                                psum[bi][:mcw, 0:512],
                                w_win[sl][:, kc * WINC + mc * 128:
                                          kc * WINC + mc * 128 + mcw],
                                xn_fm_hi[:, kc * TILE:(kc + 1) * TILE],
                                start=(kc == 0), stop=(kc == 15))
                        return last
                    p.stage("pe", wbody, reads=[wb, gb("xn_fm_hi")],
                            writes=[pbank[bi]])
                    if gf < DI:
                        q = gf // 128
                        srs = q % 4
                        p.stage("act", lambda E, srs=srs, bi=bi: E.activation(
                            out=sz_ring[srs], in_=psum[bi][:, 0:512],
                            func=AF.Silu),
                            reads=[pbank[bi]], writes=[gb(f"sz{srs}")])
                        p.dma("act", f"szst{srs}",
                              [(st_sz[t, q, :, :], sz_ring[srs])],
                              reads=[gb(f"sz{srs}")], writes=[gb("D_sz")])
                    elif gf < DI + CONV:
                        ci = (gf - DI) // 128
                        xs = xbcp_i[0] % 4
                        xbcp_i[0] += 1
                        xb2 = gb(f"xbcp{xs}")
                        p.stage("dve", lambda E, xs=xs, bi=bi: E.tensor_copy(
                            out=xbcp[xs][:, 3:515], in_=psum[bi][:, 0:512]),
                            reads=[pbank[bi]], writes=[xb2])
                        p.stage("dve", lambda E, xs=xs, ci=ci: E.tensor_copy(
                            out=xbcp[xs][:, 0:3],
                            in_=tail[:, ci * 3:ci * 3 + 3]),
                            reads=[gb("tail")], writes=[xb2])
                        cvs = ci % 2
                        cb = gb(f"cscr{cvs}")
                        p.stage("dve", lambda E, xs=xs, ci=ci, cvs=cvs:
                                E.tensor_scalar(
                                    out=conv_scr[cvs], in0=xbcp[xs][:, 0:512],
                                    scalar1=conv_w_s[:, ci * 4:ci * 4 + 1],
                                    scalar2=None, op0=ALU.mult),
                                reads=[xb2, gb("globals")], writes=[cb])
                        for k in range(1, 4):
                            p.stage("dve",
                                    lambda E, xs=xs, ci=ci, cvs=cvs, k=k:
                                    E.scalar_tensor_tensor(
                                        out=conv_scr[cvs],
                                        in0=xbcp[xs][:, k:k + 512],
                                        scalar=conv_w_s[:, ci * 4 + k:
                                                        ci * 4 + k + 1],
                                        in1=conv_scr[cvs],
                                        op0=ALU.mult, op1=ALU.add),
                                    reads=[xb2, cb, gb("globals")],
                                    writes=[cb])
                        p.stage("pool", lambda E, xs=xs, ci=ci: E.tensor_copy(
                            out=tail[:, ci * 3:ci * 3 + 3],
                            in_=xbcp[xs][:, 512:515]),
                            reads=[xb2], writes=[gb("tail")])
                        if ci < 32:
                            dst_ap = x_fm[:, ci * 512:(ci + 1) * 512]
                            dstb = gb("x_fm")
                        elif ci == 32:
                            dst_ap = b_fm[:, :]
                            dstb = gb("b_fm")
                        else:
                            dst_ap = c_fm[:, :]
                            dstb = gb("c_fm")
                        p.stage("act", lambda E, cvs=cvs, ci=ci,
                                dst_ap=dst_ap: E.activation(
                                    out=dst_ap, in_=conv_scr[cvs],
                                    func=AF.Silu,
                                    bias=conv_b_s[:, ci:ci + 1]),
                                reads=[cb, gb("globals")], writes=[dstb])
                    else:
                        p.stage("act", lambda E, bi=bi: (
                            E.activation(out=dt_scr, in_=psum[bi][:64, 0:512],
                                         func=AF.Abs, bias=dtb_s[:, :]),
                            E.activation(out=dt_fm, in_=psum[bi][:64, 0:512],
                                         func=AF.Relu, bias=dtb_s[:, :]))[-1],
                            reads=[pbank[bi], gb("globals")],
                            writes=[gb("dt_scr"), gb("dt_fm")])
                        p.stage("act", lambda E: E.activation(
                            out=dt_scr, in_=dt_scr, func=AF.Exp,
                            scale=cst_s[:64, 2:3]),
                            reads=[gb("dt_scr"), gb("globals")],
                            writes=[gb("dt_scr")])
                        p.stage("act", lambda E: E.activation(
                            out=dt_scr, in_=dt_scr, func=AF.Ln,
                            bias=cst_s[:64, 1:2]),
                            reads=[gb("dt_scr"), gb("globals")],
                            writes=[gb("dt_scr")])
                        p.stage("dve", lambda E: E.tensor_tensor(
                            out=dt_fm, in0=dt_fm, in1=dt_scr, op=ALU.add),
                            reads=[gb("dt_scr"), gb("dt_fm")],
                            writes=[gb("dt_fm")])
            p.stage("dve", lambda E: E.tensor_scalar(
                out=dacs_fm, in0=dt_fm, scalar1=a_s[:, :], scalar2=None,
                op0=ALU.mult),
                reads=[gb("dt_fm"), gb("globals")], writes=[gb("dacs_fm")])

            def csbody(E):
                last = None
                for c in range(NCH):
                    last = E.tensor_tensor_scan(
                        out=dacs_fm[:, c * 128:(c + 1) * 128],
                        data0=dacs_fm[:, c * 128:(c + 1) * 128],
                        data1=zeros64, initial=0.0, op0=ALU.add, op1=ALU.add)
                return last
            p.stage("dve", csbody, reads=[gb("z64"), gb("dacs_fm")],
                    writes=[gb("dacs_fm")])
            p.dma("act", "dacsst", [(st_dacs[t, :, :], dacs_fm)],
                  reads=[gb("dacs_fm")], writes=[gb("D_dacs")])
            for c in range(NCH):
                p.stage("pool", lambda E, c=c: E.tensor_tensor(
                    out=ld_col[:, :], in0=ld_col[:, :],
                    in1=dacs_fm[:, c * 128 + 127:c * 128 + 128], op=ALU.add),
                    reads=[gb("dacs_fm")], writes=[gb("ld")])

            for c in range(NCH):
                for g8 in range(4):
                    bi = g8 % 2

                    def xb_body(E, g8=g8, c=c, bi=bi):
                        last = None
                        for j in range(8):
                            q = g8 * 8 + j
                            last = E.transpose(
                                psum[bi][:, :].bitcast(BF16)[
                                    :, j * 128:(j + 1) * 128],
                                x_fm[:, q * 512 + c * 128:
                                     q * 512 + c * 128 + 128],
                                eye_bf[:, :])
                        return last
                    p.stage("pe", xb_body, reads=[gb("x_fm"), gb("globals")],
                            writes=[pbank[bi]])
                    p.stage("dve", lambda E, g8=g8, bi=bi: E.tensor_copy(
                        out=x_tm_c[:, g8 * 1024:(g8 + 1) * 1024],
                        in_=psum[bi][:, :].bitcast(BF16)[:, 0:1024]),
                        reads=[pbank[bi]], writes=[gb("x_tm_c")])
                p.stage("pe", lambda E, c=c: E.transpose(
                    psum[2][:, :].bitcast(BF16)[:, 0:128],
                    b_fm[:, c * 128:(c + 1) * 128], eye_bf[:, :]),
                    reads=[gb("b_fm"), gb("globals")], writes=[pbank[2]])
                p.stage("dve", lambda E: E.tensor_copy(
                    out=b_tm, in_=psum[2][:, :].bitcast(BF16)[:, 0:128]),
                    reads=[pbank[2]], writes=[gb("b_tm")])
                p.stage("dve", lambda E, c=c: E.tensor_scalar(
                    out=darg_fm, in0=dacs_fm[:, c * 128:(c + 1) * 128],
                    scalar1=cst_s[:64, 2:3],
                    scalar2=dacs_fm[:, c * 128 + 127:c * 128 + 128],
                    op0=ALU.mult, op1=ALU.add),
                    reads=[gb("dacs_fm"), gb("globals")],
                    writes=[gb("darg")])
                p.stage("pe", lambda E, c=c: (
                    E.transpose(psum[2][:, 0:64],
                                dacs_fm[:, c * 128:(c + 1) * 128],
                                eye_f[:64, :64]),
                    E.transpose(psum[2][:, 64:128],
                                dt_fm[:, c * 128:(c + 1) * 128],
                                eye_f[:64, :64]),
                    E.transpose(psum[2][:, 128:192], darg_fm,
                                eye_f[:64, :64]))[-1],
                    reads=[gb("dacs_fm"), gb("dt_fm"), gb("darg"),
                           gb("globals")],
                    writes=[pbank[2]])
                p.stage("dve", lambda E: E.tensor_copy(
                    out=dacs_tm, in_=psum[2][:, 0:64]),
                    reads=[pbank[2]], writes=[gb("dacs_tm")])
                p.stage("dve", lambda E: E.tensor_copy(
                    out=dt_tm, in_=psum[2][:, 64:128]),
                    reads=[pbank[2]], writes=[gb("dt_tm")])
                p.stage("act", lambda E: E.activation(
                    out=dec_tm, in_=psum[2][:, 128:192], func=AF.Exp),
                    reads=[pbank[2]], writes=[gb("dec_tm")])
                p.stage("pe", lambda E, c=c: E.transpose(
                    psum[3][:1, 0:64],
                    dacs_fm[:, c * 128 + 127:c * 128 + 128],
                    eye_f[:64, :64]),
                    reads=[gb("dacs_fm"), gb("globals")], writes=[pbank[3]])
                p.stage("dve", lambda E: E.tensor_copy(
                    out=smallrow[:1, 0:64], in_=psum[3][:1, 0:64]),
                    reads=[pbank[3]], writes=[gb("smallrow")])
                p.dma("act", "dalst",
                      [(st_dalast[t, c, :].unsqueeze(0), smallrow[:1, 0:64])],
                      reads=[gb("smallrow")], writes=[gb("D_dal")])
                p.stage("pe", lambda E: E.matmul(
                    psum[3][:, 64:128], ones_r[:1, 0:128],
                    smallrow[:1, 0:64], start=True, stop=True),
                    reads=[gb("smallrow"), gb("globals")], writes=[pbank[3]])
                p.stage("act", lambda E: E.activation(
                    out=cdec_bc, in_=psum[3][:, 64:128], func=AF.Exp),
                    reads=[pbank[3]], writes=[gb("cdec")])
                p.stage("pe", lambda E, c=c: E.matmul(
                    psum[6][:, 0:128], b_fm[:, c * 128:(c + 1) * 128],
                    c_fm[:, c * 128:(c + 1) * 128], start=True, stop=True),
                    reads=[gb("b_fm"), gb("c_fm")], writes=[pbank[6]])
                p.stage("dve", lambda E: E.tensor_copy(
                    out=bc_sb, in_=psum[6][:, 0:128]),
                    reads=[pbank[6]], writes=[gb("bc_sb")])
                p.dma("act", "cst",
                      [(st_c[t, c, :, :], c_fm[:, c * 128:(c + 1) * 128])],
                      reads=[gb("c_fm")], writes=[gb("D_c")])

                def dtx_b(E):
                    in1 = dt_tm.unsqueeze(2).to_broadcast([128, 64, 64])
                    return E.tensor_tensor(
                        out=dtx.rearrange("p (h d) -> p h d", d=64),
                        in0=x_tm_c.rearrange("p (h d) -> p h d", d=64),
                        in1=in1, op=ALU.mult)
                p.stage("dve", dtx_b, reads=[gb("x_tm_c"), gb("dt_tm")],
                        writes=[gb("dtx")])
                def rows_fetch(pd):
                    rs = pd % 2
                    p.dma("sp", f"rows{rs}",
                          [(rowbuf[rs][:1, (g - pd * 2) * 512:
                                       (g - pd * 2 + 1) * 512]
                            .rearrange("p (a b) -> p a b", b=128),
                            st_dacs[t, g * 4:(g + 1) * 4,
                                    c * 128:(c + 1) * 128].unsqueeze(0))
                           for g in range(pd * 2, pd * 2 + 2)],
                          writes=[gb(f"rows{rs}")], reads=[gb("D_dacs")])
                rows_fetch(0)
                rows_fetch(1)
                for g in range(16):
                    if g % 2 == 0 and g >= 4:
                        rows_fetch(g // 2)
                    rs = (g // 2) % 2
                    rb = gb(f"rows{rs}")
                    ab = 2 + (g % 2)
                    yb = 4 + (g % 2)
                    bcb = gb(f"bce{g % 2}")
                    cscb = gb(f"csc{g % 2}")
                    p.stage("pe", lambda E, g=g, ab=ab, rs=rs: E.matmul(
                        psum[ab][:, 0:512], ones_r[:1, 0:128],
                        rowbuf[rs][:1, (g % 2) * 512:(g % 2 + 1) * 512],
                        start=True, stop=True),
                        reads=[rb, gb("globals")], writes=[pbank[ab]])
                    p.stage("act", lambda E, g=g, ab=ab: E.activation(
                        out=bce_sb[g % 2], in_=psum[ab][:, 0:512],
                        func=AF.Exp),
                        reads=[pbank[ab]], writes=[bcb])

                    def csc_f(E, g=g, c=c):
                        in0 = c_fm[:, c * 128:(c + 1) * 128].unsqueeze(1) \
                            .to_broadcast([128, 4, 128])
                        return E.tensor_tensor(
                            out=csc_sb[g % 2].rearrange(
                                "p (a b) -> p a b", b=128),
                            in0=in0, in1=bce_sb[g % 2].rearrange(
                                "p (a b) -> p a b", b=128),
                            op=ALU.mult)
                    p.stage("pool", csc_f, reads=[gb("c_fm"), bcb],
                            writes=[cscb])
                    for hh in range(4):
                        h = g * 4 + hh
                        ls = hh % 2
                        lb = gb(f"lt{ls}")
                        sb_ = gb(f"stx{ls}")
                        p.stage("dve", lambda E, hh=hh, h=h, ab=ab, ls=ls:
                                E.scalar_tensor_tensor(
                                    out=lt_sb[ls],
                                    in0=psum[ab][:, hh * 128:(hh + 1) * 128],
                                    scalar=dacs_tm[:, h:h + 1],
                                    in1=m_up[:, :],
                                    op0=ALU.subtract, op1=ALU.add),
                                reads=[pbank[ab], gb("dacs_tm"),
                                       gb("globals")],
                                writes=[lb])
                        p.stage("act", lambda E, ls=ls: E.activation(
                            out=lt_sb[ls], in_=lt_sb[ls], func=AF.Exp),
                            reads=[lb], writes=[lb])
                        p.stage("dve", lambda E, ls=ls: E.tensor_tensor(
                            out=stx_sb[ls], in0=bc_sb, in1=lt_sb[ls],
                            op=ALU.mult),
                            reads=[lb, gb("bc_sb")], writes=[sb_])
                        p.stage("pe", lambda E, h=h, hh=hh, yb=yb, ls=ls,
                                g=g: (
                            E.matmul(psum[yb][:64, hh * 128:(hh + 1) * 128],
                                     dtx[:, h * 64:(h + 1) * 64],
                                     stx_sb[ls], start=True, stop=False),
                            E.matmul(psum[yb][:64, hh * 128:(hh + 1) * 128],
                                     prevs_t[:, h * 64:(h + 1) * 64],
                                     csc_sb[g % 2][:, hh * 128:
                                                   (hh + 1) * 128],
                                     start=False, stop=True))[-1],
                            reads=[gb("dtx"), sb_, cscb, gb("prevs")],
                            writes=[pbank[yb]])

                    def yev(E, g=g, c=c, yb=yb):
                        last = None
                        for hh in range(4):
                            h = g * 4 + hh
                            q = h // 2
                            r0 = (h % 2) * 64
                            last = E.scalar_tensor_tensor(
                                out=ylc[c % 2][r0:r0 + 64,
                                               q * 128:(q + 1) * 128],
                                in0=x_fm[r0:r0 + 64,
                                         q * 512 + c * 128:
                                         q * 512 + c * 128 + 128],
                                scalar=dbc_s[r0:r0 + 64, h:h + 1],
                                in1=psum[yb][:64, hh * 128:(hh + 1) * 128],
                                op0=ALU.mult, op1=ALU.add)
                        return last
                    p.stage("dve", yev,
                            reads=[gb("x_fm"), gb("globals"), pbank[yb]],
                            writes=[gb("ylc")])
                p.dma("act", f"ylst{c % 2}",
                      [(st_yl[t, c, :, :], ylc[c % 2])],
                      reads=[gb("ylc")], writes=[gb("D_yl")])

                def ddtx_b(E):
                    in1 = dec_tm.unsqueeze(2).to_broadcast([128, 64, 64])
                    return E.tensor_tensor(
                        out=dtx.rearrange("p (h d) -> p h d", d=64),
                        in0=dtx.rearrange("p (h d) -> p h d", d=64),
                        in1=in1, op=ALU.mult)
                p.stage("dve", ddtx_b, reads=[gb("dec_tm"), gb("dtx")],
                        writes=[gb("dtx")])
                for sg_ in range(8):
                    bi = sg_ % 2
                    p.stage("pe", lambda E, sg_=sg_, bi=bi: E.matmul(
                        psum[bi][:, 0:512], b_tm,
                        dtx[:, sg_ * 512:(sg_ + 1) * 512],
                        start=True, stop=True),
                        reads=[gb("b_tm"), gb("dtx")], writes=[pbank[bi]])

                    def scm(E, sg_=sg_):
                        in1 = cdec_bc[:, sg_ * 8:(sg_ + 1) * 8] \
                            .unsqueeze(2).to_broadcast([128, 8, 64])
                        return E.tensor_tensor(
                            out=scan_scr.rearrange("p (a b) -> p a b", b=64),
                            in0=prevs_t[:, sg_ * 512:(sg_ + 1) * 512]
                            .rearrange("p (a b) -> p a b", b=64),
                            in1=in1, op=ALU.mult)
                    p.stage("pool", scm, reads=[gb("prevs"), gb("cdec")],
                            writes=[gb("scan")])
                    p.stage("dve", lambda E, sg_=sg_, bi=bi: E.tensor_tensor(
                        out=prevs_t[:, sg_ * 512:(sg_ + 1) * 512],
                        in0=scan_scr, in1=psum[bi][:, 0:512], op=ALU.add),
                        reads=[gb("scan"), pbank[bi]], writes=[gb("prevs")])
            p.dma("act", "xnst",
                  [(st_xn[t, kc, :, :],
                    xn_fm_hi[:, kc * TILE:(kc + 1) * TILE])
                   for kc in range(16)],
                  reads=[gb("xn_fm_hi")], writes=[gb("D_xn")])
            emit_transcode(118 if t > 0 else 100)
        emit_transcode(10**9)

        p.stage("pe", lambda E: E.transpose(
            psum[0][:1, 0:64], ld_col[:, :], eye_f[:64, :64]),
            reads=[gb("ld"), gb("globals")], writes=[pbank[0]])
        p.stage("dve", lambda E: E.tensor_copy(out=zrow[0:1, :],
                                               in_=psum[0][:1, 0:64]),
                reads=[pbank[0]], writes=[gb("ldbf")])
        p.dma("act", "ccw", [(cc_in[:, 0:DI], prevs_t[:, :]),
                             (cc_in[:, 4096:4160], zrow[:, :])],
              reads=[gb("prevs"), gb("ldbf")])
        cc_sem = ctx.enter_context(nc.semaphore("cc_sem"))
        ccw = p.dma_sems["ccw"]

        @block.gpsimd
        def _(E):
            E.wait_ge(ccw[0], ccw[1])
            E.collective_compute(
                "AllGather", ALU.bypass,
                replica_groups=[list(range(n_cores))],
                ins=[cc_in[:, :]], outs=[cc_out[:, :]]).then_inc(cc_sem)
        cc_tok = (cc_sem, 1)
        p.barrier(extra=[cc_tok])
        p.dma("pool", "ldg",
              [(ldg[:n_cores, :].unsqueeze(1),
                cc_out[:, 4096:4160].rearrange("(j q) c -> j q c", q=128)[
                    :, 0:1, :])],
              writes=[gb("ldg")], deps=[cc_tok])
        p.stage("pe", lambda E: E.matmul(
            psum[0][:64, 0:8], ldg[:n_cores, :], selT_s[:n_cores, :],
            start=True, stop=True),
            reads=[gb("ldg"), gb("globals")], writes=[pbank[0]])
        p.stage("dve", lambda E: E.tensor_tensor(
            out=warg, in0=psum[0][:64, 0:8], in1=maskT_s[:, :], op=ALU.add),
            reads=[pbank[0], gb("globals")], writes=[gb("warg")])
        p.stage("act", lambda E: E.activation(
            out=warg, in_=warg, func=AF.Exp),
            reads=[gb("warg")], writes=[gb("warg")])
        p.stage("pool", lambda E: E.memset(fold_scr, 0.0),
                writes=[gb("fold")])
        for j in range(n_cores):
            gsl = j % 2
            gbj = gb(f"gthr{gsl}")
            p.dma("sp", f"gth{gsl}",
                  [(gthr[gsl], cc_out[j * 128:(j + 1) * 128, 0:DI])],
                  writes=[gbj], deps=[cc_tok])
            p.stage("pe", lambda E, j=j: E.transpose(
                psum[1][:1, 0:64], warg[:, j:j + 1], eye_f[:64, :64]),
                reads=[gb("warg"), gb("globals")], writes=[pbank[1]])
            p.stage("dve", lambda E: E.tensor_copy(
                out=smallrow[:1, 0:64], in_=psum[1][:1, 0:64]),
                reads=[pbank[1]], writes=[gb("smallrow")])
            for sl4 in range(8):
                bi = 2 + sl4 % 2

                def bcw(E, sl4=sl4, bi=bi):
                    rhs = smallrow[:1, sl4 * 8:(sl4 + 1) * 8].unsqueeze(2) \
                        .to_broadcast([1, 8, 64])
                    return E.matmul(psum[bi][:, 0:512], ones_r[:1, 0:128],
                                    rhs, start=True, stop=True)
                p.stage("pe", bcw, reads=[gb("smallrow"), gb("globals")],
                        writes=[pbank[bi]])
                p.stage("dve", lambda E, sl4=sl4, gsl=gsl, bi=bi:
                        E.tensor_tensor(
                            out=ex_scr,
                            in0=gthr[gsl][:, sl4 * 512:(sl4 + 1) * 512],
                            in1=psum[bi][:, 0:512], op=ALU.mult),
                        reads=[gbj, pbank[bi]], writes=[gb("exscr")])
                p.stage("pool", lambda E, sl4=sl4: E.tensor_tensor(
                    out=fold_scr[:, sl4 * 512:(sl4 + 1) * 512],
                    in0=fold_scr[:, sl4 * 512:(sl4 + 1) * 512],
                    in1=ex_scr, op=ALU.add),
                    reads=[gb("exscr"), gb("fold")], writes=[gb("fold")])
        p.stage("dve", lambda E: E.tensor_copy(out=inc_bf, in_=fold_scr),
                reads=[gb("fold")], writes=[gb("prevs")])

        for t in range(NT):
            szb = gb("szB")
            p.dma("sp", "szld",
                  [(sz_b[:, q * TILE:(q + 1) * TILE], st_sz[t, q, :, :])
                   for q in range(32)],
                  writes=[szb], reads=[gb("D_sz")])
            for c in range(NCH):
                ylb = gb(f"ylB{c % 2}")
                p.dma("sp", f"ylld{c % 2}",
                      [(yl_b[c % 2], st_yl[t, c, :, :])],
                      writes=[ylb], reads=[gb("D_yl")])
                cb2 = gb(f"cB{c % 2}")
                p.dma("sp", f"cld{c % 2}", [(c_b[c % 2], st_c[t, c, :, :])],
                      writes=[cb2], reads=[gb("D_c")])
                def rows_fetch_b(pd):
                    rs = pd % 2
                    p.dma("sp", f"rowsB{rs}",
                          [(rowb1[rs][:1, (g - pd * 2) * 512:
                                      (g - pd * 2 + 1) * 512]
                            .rearrange("p (a b) -> p a b", b=128),
                            st_dacs[t, g * 4:(g + 1) * 4,
                                    c * 128:(c + 1) * 128].unsqueeze(0))
                           for g in range(pd * 2, pd * 2 + 2)],
                          writes=[gb(f"rowsB{rs}")], reads=[gb("D_dacs")])
                rows_fetch_b(0)
                rows_fetch_b(1)
                p.dma("sp", "srow64",
                      [(srow64[:1, :],
                        st_dalast[t, c, :].unsqueeze(0))],
                      writes=[gb("srow64")], reads=[gb("D_dal")])
                p.stage("pe", lambda E: E.matmul(
                    psum[7][:, 0:64], ones_r[:1, 0:128], srow64[:1, :],
                    start=True, stop=True),
                    reads=[gb("srow64"), gb("globals")], writes=[pbank[7]])
                p.stage("act", lambda E: E.activation(
                    out=cdec_b1, in_=psum[7][:, 0:64], func=AF.Exp),
                    reads=[pbank[7]], writes=[gb("cdecB")])
                for g in range(16):
                    if g % 2 == 0 and g >= 4:
                        rows_fetch_b(g // 2)
                    rs = (g // 2) % 2
                    rb = gb(f"rowsB{rs}")
                    ab = 2 + g % 2
                    yb = 4 + g % 2
                    bcb = gb(f"bceB{g % 2}")
                    cscb = gb(f"cscB{g % 2}")
                    p.stage("pe", lambda E, g=g, ab=ab, rs=rs: E.matmul(
                        psum[ab][:, 0:512], ones_r[:1, 0:128],
                        rowb1[rs][:1, (g % 2) * 512:(g % 2 + 1) * 512],
                        start=True, stop=True),
                        reads=[rb, gb("globals")], writes=[pbank[ab]])
                    p.stage("act", lambda E, g=g, ab=ab: E.activation(
                        out=bce_b1[g % 2], in_=psum[ab][:, 0:512],
                        func=AF.Exp),
                        reads=[pbank[ab]], writes=[bcb])

                    def cscf(E, g=g, c=c):
                        in0 = c_b[c % 2].unsqueeze(1).to_broadcast(
                            [128, 4, 128])
                        return E.tensor_tensor(
                            out=csc_b1[g % 2].rearrange(
                                "p (a b) -> p a b", b=128),
                            in0=in0,
                            in1=bce_b1[g % 2].rearrange(
                                "p (a b) -> p a b", b=128),
                            op=ALU.mult)
                    p.stage("pool", cscf, reads=[cb2, bcb], writes=[cscb])

                    def ycm(E, g=g, yb=yb):
                        last = None
                        for hh in range(4):
                            h = g * 4 + hh
                            last = E.matmul(
                                psum[yb][:64, hh * 128:(hh + 1) * 128],
                                inc_bf[:, h * 64:(h + 1) * 64],
                                csc_b1[g % 2][:, hh * 128:(hh + 1) * 128],
                                start=True, stop=True)
                        return last
                    p.stage("pe", ycm, reads=[gb("prevs"), cscb],
                            writes=[pbank[yb]])

                    def gatef(E, g=g, c=c, yb=yb):
                        last = None
                        for hh in range(4):
                            h = g * 4 + hh
                            q = h // 2
                            r0 = (h % 2) * 64
                            last = E.tensor_tensor(
                                out=yg_sb[r0:r0 + 64, q * 128:(q + 1) * 128],
                                in0=yl_b[c % 2][r0:r0 + 64,
                                                q * 128:(q + 1) * 128],
                                in1=psum[yb][:64, hh * 128:(hh + 1) * 128],
                                op=ALU.add)
                        return last
                    p.stage("dve", gatef, reads=[ylb, pbank[yb]],
                            writes=[gb("yg")])
                for q in range(32):
                    p.stage("dve", lambda E, q=q, c=c: E.tensor_tensor(
                        out=yg_sb[:, q * 128:(q + 1) * 128],
                        in0=yg_sb[:, q * 128:(q + 1) * 128],
                        in1=sz_b[:, q * TILE + c * 128:
                                 q * TILE + c * 128 + 128],
                        op=ALU.mult),
                        reads=[gb("yg"), szb], writes=[gb("yg")])
                    p.stage("act", lambda E, q=q: E.activation(
                        out=ygsq, in_=yg_sb[:, q * 128:(q + 1) * 128],
                        func=AF.Square),
                        reads=[gb("yg")], writes=[gb("ygsq")])
                    p.stage("pe", lambda E, q=q: E.matmul(
                        psum[6][:1, 0:128], ones_c[:, 0:1], ygsq,
                        start=(q == 0), stop=(q == 31)),
                        reads=[gb("ygsq"), gb("globals")], writes=[pbank[6]])
                p.stage("act", lambda E: E.activation(
                    out=rstd_r[:1, :], in_=psum[6][:1, 0:128], func=AF.Ln,
                    bias=cst_s[0:1, 0:1], scale=cst_s[0:1, 5:6]),
                    reads=[pbank[6], gb("globals")], writes=[gb("rstd")])
                p.stage("act", lambda E: E.activation(
                    out=rstd_r[:1, :], in_=rstd_r[:1, :], func=AF.Exp,
                    scale=cst_s[0:1, 3:4]),
                    reads=[gb("rstd"), gb("globals")], writes=[gb("rstd")])
                p.stage("pe", lambda E: E.matmul(
                    psum[6][:, 0:128], ones_r[:1, 0:128], rstd_r[:1, :],
                    start=True, stop=True),
                    reads=[gb("rstd"), gb("globals")], writes=[pbank[6]])
                for q in range(32):
                    p.stage("dve", lambda E, q=q, c=c: E.tensor_tensor(
                        out=yn_b[:, q * TILE + c * 128:
                                 q * TILE + c * 128 + 128],
                        in0=yg_sb[:, q * 128:(q + 1) * 128],
                        in1=psum[6][:, 0:128], op=ALU.mult),
                        reads=[gb("yg"), pbank[6]], writes=[gb("ynB")])

                def incsc(E):
                    in1 = cdec_b1.unsqueeze(2).to_broadcast([128, 64, 64])
                    return E.tensor_tensor(
                        out=inc_bf.rearrange("p (h d) -> p h d", d=64),
                        in0=inc_bf.rearrange("p (h d) -> p h d", d=64),
                        in1=in1, op=ALU.mult)
                p.stage("dve", incsc, reads=[gb("cdecB"), gb("prevs")],
                        writes=[gb("prevs")])
            for w in range(8):
                sl = w % 2
                wob = gb(f"woB{sl}")
                p.dma("sp", f"wo{sl}",
                      [(wo_win[sl][:, kc * 256:(kc + 1) * 256],
                        wbf_out[kc, :, w * 256:(w + 1) * 256])
                       for kc in range(32)],
                      writes=[wob], reads=[gb("D_wo")])
                for mc in range(2):
                    bi = sl * 2 + mc

                    def wom(E, w=w, mc=mc, sl=sl, bi=bi):
                        last = None
                        for kc in range(32):
                            last = E.matmul(
                                psum[bi][:, 0:512],
                                wo_win[sl][:, kc * 256 + mc * 128:
                                           kc * 256 + mc * 128 + 128],
                                yn_b[:, kc * TILE:(kc + 1) * TILE],
                                start=(kc == 0), stop=(kc == 31))
                        return last
                    p.stage("pe", wom, reads=[wob, gb("ynB")],
                            writes=[pbank[bi]])
                    srs = (w * 2 + mc) % 4
                    p.stage("dve", lambda E, bi=bi, srs=srs: E.tensor_copy(
                        out=s_ev[srs], in_=psum[bi][:, 0:512]),
                        reads=[pbank[bi]], writes=[gb(f"sev{srs}")])
                    p.dma("act", f"sst{srs}",
                          [(st_s[t, w * 2 + mc, :, :], s_ev[srs])],
                          reads=[gb(f"sev{srs}")], writes=[gb("D_s")])

        p.barrier()
        for t in range(NT):
            xnb = gb("xnB2")
            p.dma("sp", "xnld",
                  [(xn_b2[:, kc * TILE:(kc + 1) * TILE], st_xn[t, kc, :, :])
                   for kc in range(16)],
                  writes=[xnb], reads=[gb("D_xn")])
            fb = gb("ffnB2")
            p.stage("dve", lambda E: E.tensor_copy(out=ffn_acc, in_=xn_b2),
                    reads=[xnb], writes=[fb])
            for e in range(NE + NSH):
                routed = e < NE
                if routed:
                    p.dma("sp", "cmbr",
                          [(cmb_row[:1, :], st_cmb[t, e, :].unsqueeze(0))],
                          writes=[gb("cmbrow")], reads=[gb("D_cmb")])
                    p.stage("pe", lambda E: E.matmul(
                        psum[0][:, 0:512], ones_r[:1, 0:128], cmb_row[:1, :],
                        start=True, stop=True),
                        reads=[gb("cmbrow"), gb("globals")],
                        writes=[pbank[0]])
                    p.stage("dve", lambda E: E.tensor_copy(
                        out=cmb_bc, in_=psum[0][:, 0:512]),
                        reads=[pbank[0]], writes=[gb("cmbbc")])
                for half in range(2):
                    for kc in range(16):
                        sl = kc % 4
                        wsb = gb(f"wg{sl}")
                        p.dma("sp", f"wg{sl}",
                              [(wgw1[sl], wbf_gw1[e, half, kc, :, :])],
                              writes=[wsb], reads=[gb("D_gw1")])

                        def gwm(E, kc=kc, sl=sl):
                            last = None
                            for mc in range(4):
                                for off, b0 in ((0, 0), (512, 4)):
                                    last = E.matmul(
                                        psum[b0 + mc][:, 0:512],
                                        wgw1[sl][:, off + mc * 128:
                                                 off + mc * 128 + 128],
                                        xn_b2[:, kc * TILE:(kc + 1) * TILE],
                                        start=(kc == 0), stop=(kc == 15))
                            return last
                        p.stage("pe", gwm, reads=[wsb, xnb],
                                writes=[pbank[i] for i in range(8)])
                    for mc in range(4):
                        bcol = (e if routed else e - NE) * 8 + half * 4 + mc
                        gbias = rgb_s if routed else sgb_s
                        bias1 = rb1_s if routed else sb1_s
                        p.stage("act", lambda E, mc=mc, bcol=bcol,
                                gbias=gbias: E.activation(
                                    out=gw_scr, in_=psum[mc][:, 0:512],
                                    func=AF.Gelu,
                                    bias=gbias[:, bcol:bcol + 1]),
                                reads=[pbank[mc], gb("globals")],
                                writes=[gb("gwscr")])
                        p.stage("act", lambda E, mc=mc, bcol=bcol,
                                bias1=bias1: E.activation(
                                    out=w1_scr, in_=psum[4 + mc][:, 0:512],
                                    func=AF.Identity,
                                    bias=bias1[:, bcol:bcol + 1]),
                                reads=[pbank[4 + mc], gb("globals")],
                                writes=[gb("w1scr")])
                        p.stage("dve", lambda E: E.tensor_tensor(
                            out=gw_scr, in0=gw_scr, in1=w1_scr, op=ALU.mult),
                            reads=[gb("gwscr"), gb("w1scr")],
                            writes=[gb("gwscr")])
                        hq = half * 4 + mc
                        if routed:
                            p.stage("dve", lambda E, hq=hq: E.tensor_tensor(
                                out=h_b2[:, hq * TILE:(hq + 1) * TILE],
                                in0=gw_scr, in1=cmb_bc, op=ALU.mult),
                                reads=[gb("gwscr"), gb("cmbbc")],
                                writes=[gb("hB2")])
                        else:
                            p.stage("dve", lambda E, hq=hq: E.tensor_copy(
                                out=h_b2[:, hq * TILE:(hq + 1) * TILE],
                                in_=gw_scr),
                                reads=[gb("gwscr")], writes=[gb("hB2")])
                w2b = gb("w2full")
                p.dma("sp", "w2ld",
                      [(w2full[:, kc * D:(kc + 1) * D], wbf_w2[e, kc, :, :])
                       for kc in range(8)],
                      writes=[w2b], reads=[gb("D_w2")])
                for ocg in range(4):
                    b0 = 0 if ocg % 2 == 0 else 4
                    bsrc = rb2_in if routed else sb2_in
                    boff = (e if routed else e - NE) * 2048 + ocg * 512
                    p.dma("sp", "rb2r",
                          [(rb2_row[:1, :512], bsrc[:1, boff:boff + 512])],
                          writes=[gb("rb2row")])

                    def w2m(E, ocg=ocg, b0=b0):
                        last = None
                        for kc in range(8):
                            for oo in range(4):
                                oc = ocg * 4 + oo
                                last = E.matmul(
                                    psum[b0 + oo][:, 0:512],
                                    w2full[:, kc * D + oc * 128:
                                           kc * D + oc * 128 + 128],
                                    h_b2[:, kc * TILE:(kc + 1) * TILE],
                                    start=(kc == 0), stop=False)
                        return last
                    p.stage("pe", w2m, reads=[w2b, gb("hB2")],
                            writes=[pbank[b0 + i] for i in range(4)])

                    def bm(E, b0=b0, routed=routed):
                        last = None
                        for oo in range(4):
                            lh = rb2_row[:1, oo * 128:(oo + 1) * 128]
                            rh = cmb_bc[0:1, :] if routed else ones_r[:1, :]
                            last = E.matmul(psum[b0 + oo][:, 0:512], lh, rh,
                                            start=False, stop=True)
                        return last
                    p.stage("pe", bm,
                            reads=[gb("rb2row"), gb("cmbbc"), gb("globals")],
                            writes=[pbank[b0 + i] for i in range(4)])
                    for oo in range(4):
                        oc = ocg * 4 + oo
                        bi = b0 + oo
                        p.stage("dve", lambda E, oc=oc, bi=bi:
                                E.tensor_tensor(
                                    out=ffn_acc[:, oc * TILE:(oc + 1) * TILE],
                                    in0=ffn_acc[:, oc * TILE:(oc + 1) * TILE],
                                    in1=psum[bi][:, 0:512], op=ALU.add),
                                reads=[pbank[bi], fb], writes=[fb])
            for tc in range(NCH):
                ssl = tc % 2
                slb = gb(f"sld{ssl}")
                p.dma("sp", f"sld{ssl}",
                      [(s_ld[ssl][:, kc * 128:(kc + 1) * 128],
                        st_s[t, kc, :, tc * 128:(tc + 1) * 128])
                       for kc in range(16)],
                      writes=[slb], reads=[gb("D_s")])
                xsl = tc % 2
                xlb = gb(f"xB2{xsl}")
                p.dma("sp", f"xld{xsl}",
                      [(x_ld[xsl], x_in[t * TILE + tc * 128:
                                        t * TILE + tc * 128 + 128, :])],
                      writes=[xlb])
                for g8 in range(2):
                    def strb(E, g8=g8, ssl=ssl):
                        last = None
                        for j in range(8):
                            kc = g8 * 8 + j
                            last = E.transpose(
                                psum[g8][:, :].bitcast(BF16)[
                                    :, j * 128:(j + 1) * 128],
                                s_ld[ssl][:, kc * 128:(kc + 1) * 128],
                                eye_bf[:, :])
                        return last
                    p.stage("pe", strb, reads=[slb, gb("globals")],
                            writes=[pbank[g8]])
                    p.stage("dve", lambda E, g8=g8: E.tensor_copy(
                        out=s_tm[:, g8 * 1024:(g8 + 1) * 1024],
                        in_=psum[g8][:, :].bitcast(BF16)[:, 0:1024]),
                        reads=[pbank[g8]], writes=[gb("stm")])
                for g4 in range(4):
                    def ftrb(E, g4=g4, tc=tc):
                        last = None
                        for j in range(4):
                            kc = g4 * 4 + j
                            last = E.transpose(
                                psum[2 + g4][:, j * 128:(j + 1) * 128],
                                ffn_acc[:, kc * TILE + tc * 128:
                                        kc * TILE + tc * 128 + 128],
                                eye_f[:, :])
                        return last
                    p.stage("pe", ftrb, reads=[fb, gb("globals")],
                            writes=[pbank[2 + g4]])
                    p.stage("act", lambda E, g4=g4: E.activation(
                        out=g_tm[:, g4 * 512:(g4 + 1) * 512],
                        in_=psum[2 + g4][:, 0:512], func=AF.Silu),
                        reads=[pbank[2 + g4]], writes=[gb("gtm")])
                p.stage("dve", lambda E: E.tensor_tensor(
                    out=s_tm, in0=s_tm, in1=g_tm, op=ALU.mult),
                    reads=[gb("stm"), gb("gtm")], writes=[gb("stm")])
                osl = tc % 2
                ob = gb(f"outB{osl}")
                p.stage("dve", lambda E, osl=osl, xsl=xsl: E.tensor_tensor(
                    out=out_b[osl], in0=s_tm, in1=x_ld[xsl], op=ALU.add),
                    reads=[gb("stm"), xlb], writes=[ob])
                p.dma("act", f"ost{osl}",
                      [(out_t[t * TILE + tc * 128:t * TILE + tc * 128 + 128,
                              :], out_b[osl])],
                      reads=[ob])

        ost = [p.dma_sems.get(f"ost{i}") for i in range(2)]

        @block.sync
        def _(E):
            for o in ost:
                if o is not None:
                    E.wait_ge(o[0], o[1])

        print(f"[build] stages={p.nstages}")
    return nc


def _reshape_inputs(inp):
    o = {}
    cw = np.asarray(inp["L0_attn_conv_w"], np.float32)
    o["conv_w_rs"] = np.ascontiguousarray(
        cw.reshape(34, 128, 4).transpose(1, 0, 2).reshape(128, 136))
    o["conv_b_rs"] = np.ascontiguousarray(
        np.asarray(inp["L0_attn_conv_b"], np.float32).reshape(34, 128).T)
    o["a_col"] = np.asarray(inp["L0_attn_A"], np.float32).reshape(64, 1)
    o["dtb_col"] = np.asarray(inp["L0_attn_dt_bias"],
                              np.float32).reshape(64, 1)
    o["d_bc"] = np.ascontiguousarray(np.broadcast_to(
        np.asarray(inp["L0_attn_D"], np.float32)[None, :], (128, 64)))
    o["norm_w_rs"] = np.ascontiguousarray(
        np.asarray(inp["L0_attn_norm_w"], np.float32).reshape(32, 128).T)
    o["wa_rs"] = np.ascontiguousarray(
        np.asarray(inp["L0_ffn_wa"], np.float32).reshape(16, 128, 16)
        .transpose(1, 0, 2).reshape(128, 256))
    for nm, key, ne in (("rgb_rs", "L0_ffn_rgb", NE),
                        ("rb1_rs", "L0_ffn_rb1", NE),
                        ("sgb_rs", "L0_ffn_sgb", NSH),
                        ("sb1_rs", "L0_ffn_sb1", NSH)):
        v = np.asarray(inp[key], np.float32).reshape(ne, 8, 128)
        o[nm] = np.ascontiguousarray(v.transpose(2, 0, 1).reshape(128, ne * 8))
    o["rb2_rs"] = np.ascontiguousarray(
        np.asarray(inp["L0_ffn_rb2"], np.float32).reshape(1, NE * 16 * 128))
    o["sb2_rs"] = np.ascontiguousarray(
        np.asarray(inp["L0_ffn_sb2"], np.float32).reshape(1, NSH * 16 * 128))
    o["eye_bf"] = np.eye(128, dtype=NPBF)
    o["eye_f"] = np.eye(128, dtype=np.float32)
    mu = np.zeros((128, 128), np.float32)
    mu[np.tril_indices(128, -1)] = MASK_NEG
    o["m_up"] = mu
    o["ones_in"] = np.ones((128, 8), np.float32)
    o["cst"] = np.ascontiguousarray(np.broadcast_to(np.array(
        [EPS, 1.0, -1.0, -0.5, 1.0 / D, 1.0 / DI, -10.0, 0.0],
        np.float32)[None, :], (128, 8)))
    o["onesr_in"] = np.ones((1, 512), np.float32)
    return o


def kernel(**inputs):
    inp = {k: np.asarray(v) for k, v in inputs.items()}
    x_full = np.ascontiguousarray(
        inp["L0_attn_hidden_states"].reshape(B * S, D).astype(np.float32,
                                                              copy=False))
    shared = _reshape_inputs(inp)
    shared["w_in"] = np.ascontiguousarray(
        inp["L0_attn_W_in"].astype(np.float32, copy=False))
    shared["w_out"] = np.ascontiguousarray(
        inp["L0_attn_W_out"].astype(np.float32, copy=False))
    for nm, key in (("rg", "L0_ffn_rg"), ("rw1", "L0_ffn_rw1"),
                    ("rw2", "L0_ffn_rw2"), ("sg", "L0_ffn_sg"),
                    ("sw1", "L0_ffn_sw1"), ("sw2", "L0_ffn_sw2")):
        shared[nm] = np.ascontiguousarray(inp[key].astype(np.float32,
                                                          copy=False))
    nc = build_bass(n_tiles=NT_BUILD)
    in_maps = []
    for c in range(NC):
        m = dict(shared)
        m["x"] = x_full[c * TOK:(c + 1) * TOK]
        if c % 4 == 0:
            m["xhalo"] = np.zeros((4, D), np.float32)
        else:
            m["xhalo"] = np.ascontiguousarray(np.concatenate(
                [x_full[c * TOK - 3:c * TOK], np.zeros((1, D), np.float32)],
                0))
        selT = np.zeros((8, 8), np.float32)
        maskT = np.full((64, 8), MASK_NEG, np.float32)
        base = (c // 4) * 4
        for j in range(base, c):
            maskT[:, j] = 0.0
            for i in range(j + 1, c):
                selT[i, j] = 1.0
        m["selT"] = selT
        m["maskT"] = maskT
        in_maps.append(m)
    res = run_bass_kernel_spmd(nc, in_maps, core_ids=list(range(NC)))
    out = np.concatenate([res.results[c]["out"] for c in range(NC)], axis=0)
    return out.reshape(B, S, D)


# version 04 (final)
# speedup vs baseline: 1.3262x; 1.3262x over previous
import numpy as np
import ml_dtypes
from contextlib import ExitStack

import concourse.bass as bass
import concourse.mybir as mybir
from concourse.bass_utils import run_bass_kernel_spmd

F32 = mybir.dt.float32
BF16 = mybir.dt.bfloat16
AF = mybir.ActivationFunctionType
ALU = mybir.AluOpType
AXL = mybir.AxisListType
NPBF = ml_dtypes.bfloat16

B, S, D = 2, 16384, 2048
DI, DS, NH, HD = 4096, 128, 64, 64
CONV = 4352
PROJ = 8512
CH = 128
EPS = 1e-5
NC = 8
TOK = B * S // NC
TILE = 512
NCH = TILE // CH
NE, NSH, FF = 16, 2, 1024
MASK_NEG = -30000.0
WINC = 256
NWIN = PROJ // WINC + (1 if PROJ % WINC else 0)
NT_BUILD = 8

ENG2BLK = {"pe": "tensor", "dve": "vector", "act": "scalar", "pool": "gpsimd",
           "sp": "sync"}


class Buf:
    __slots__ = ("w", "rs", "name")

    def __init__(self, name):
        self.name, self.w, self.rs = name, None, []


class Prog:
    def __init__(self, nc, ctx, block):
        self.nc, self.ctx, self.block = nc, ctx, block
        self.es = {}
        for e in ENG2BLK:
            self.es[e] = [ctx.enter_context(nc.semaphore(f"es_{e}")), 0]
        self.dma_sems = {}
        self.waited = {e: {} for e in ENG2BLK}
        self.nstages = 0
        self.bufs = {}

    def gb(self, name):
        if name not in self.bufs:
            self.bufs[name] = Buf(name)
        return self.bufs[name]

    def dma_sem(self, name):
        if name not in self.dma_sems:
            self.dma_sems[name] = [
                self.ctx.enter_context(self.nc.semaphore(f"dm_{name}")), 0]
        return self.dma_sems[name]

    def _waits(self, E, en, deps):
        best = {}
        for (h, v) in deps:
            k = id(h)
            if v > best.get(k, (0, None))[0]:
                best[k] = (v, h)
        for k, (v, h) in best.items():
            if self.waited[en].get(k, -1) < v:
                E.wait_ge(h, v)
                self.waited[en][k] = v

    def _deps(self, reads, writes, deps):
        dl = list(deps)
        for b in reads:
            if b.w is not None:
                dl.append(b.w)
        for b in writes:
            if b.w is not None:
                dl.append(b.w)
            dl.extend(b.rs)
        return dl

    def _book(self, tok, reads, writes):
        for b in reads:
            b.rs.append(tok)
        for b in writes:
            b.w, b.rs = tok, []

    def stage(self, en, body, reads=(), writes=(), deps=()):
        dl = self._deps(reads, writes, deps)
        sem = self.es[en]
        dec = getattr(self.block, ENG2BLK[en])

        @dec
        def _(E):
            self._waits(E, en, dl)
            last = body(E)
            assert last is not None
            last.then_inc(sem[0], 1)
        sem[1] += 1
        tok = (sem[0], sem[1])
        self._book(tok, reads, writes)
        self.nstages += 1
        return tok

    def barrier(self, extra=()):
        toks = [(s[0], s[1]) for s in self.es.values()]
        toks += [(s[0], s[1]) for s in self.dma_sems.values()]
        toks += list(extra)
        for en in ENG2BLK:
            dec = getattr(self.block, ENG2BLK[en])

            @dec
            def _(E, en=en):
                self._waits(E, en, toks)
        for b in self.bufs.values():
            b.w, b.rs = None, []

    def dma(self, qn, slot, pairs, reads=(), writes=(), deps=()):
        dl = self._deps(reads, writes, deps)
        sem = self.dma_sem(slot)
        dec = getattr(self.block, ENG2BLK[qn])

        @dec
        def _(E):
            self._waits(E, qn, dl)
            for pr in pairs:
                kw = pr[2] if len(pr) > 2 else {}
                E.dma_start(out=pr[0], in_=pr[1], **kw).then_inc(sem[0], 16)
        sem[1] += 16 * len(pairs)
        tok = (sem[0], sem[1])
        self._book(tok, reads, writes)
        return tok


class Arena:

    def __init__(self, master, total_f32):
        self.m = master
        self.total = total_f32
        self.off = 0
        self.marks = []

    def push(self):
        self.marks.append(self.off)

    def pop(self):
        self.off = self.marks.pop()

    def f32(self, n, parts=128):
        o = self.off
        self.off += n
        assert self.off <= self.total, f"arena overflow {self.off}>{self.total}"
        return self.m[0:parts, o:o + n]

    def bf16(self, n, parts=128):
        o = self.off
        self.off += (n + 1) // 2
        assert self.off <= self.total, f"arena overflow {self.off}>{self.total}"
        return self.m[0:parts, o:o + (n + 1) // 2].bitcast(BF16)


def build_bass(n_tiles=8, n_cores=NC):
    NT = n_tiles
    nc = bass.Bass("TRN2", target_bir_lowering=False)

    def din(name, shape, dt=F32):
        return nc.dram_tensor(name, shape, dt, kind="ExternalInput").ap()

    x_in = din("x", [TOK, D])
    xhalo = din("xhalo", [4, D])
    w_in = din("w_in", [D, PROJ])
    w_out = din("w_out", [DI, D])
    conv_w_rs = din("conv_w_rs", [128, 34 * 4])
    conv_b_rs = din("conv_b_rs", [128, 34])
    a_col = din("a_col", [64, 1])
    dtb_col = din("dtb_col", [64, 1])
    d_bc = din("d_bc", [128, 64])
    norm_w_rs = din("norm_w_rs", [128, 32])
    wa_rs = din("wa_rs", [128, 256])
    rg_in = din("rg", [NE, D, FF])
    rw1_in = din("rw1", [NE, D, FF])
    rw2_in = din("rw2", [NE, FF, D])
    sg_in = din("sg", [NSH, D, FF])
    sw1_in = din("sw1", [NSH, D, FF])
    sw2_in = din("sw2", [NSH, FF, D])
    rgb_rs = din("rgb_rs", [128, NE * 8])
    rb1_rs = din("rb1_rs", [128, NE * 8])
    sgb_rs = din("sgb_rs", [128, NSH * 8])
    sb1_rs = din("sb1_rs", [128, NSH * 8])
    rb2_in = din("rb2_rs", [1, NE * 16 * 128])
    sb2_in = din("sb2_rs", [1, NSH * 16 * 128])
    eye_bf_in = din("eye_bf", [128, 128], BF16)
    eye_f_in = din("eye_f", [128, 128])
    m_up_in = din("m_up", [128, 128])
    ones_in = din("ones_in", [128, 8])
    onesr_in = din("onesr_in", [1, 512])
    cst_in = din("cst", [128, 8])
    selT_in = din("selT", [8, 8])
    maskT_in = din("maskT", [64, 8])
    out_t = nc.dram_tensor("out", [TOK, D], F32, kind="ExternalOutput").ap()

    wbf_gw1 = nc.dram_tensor("wbf_gw1", [NE + NSH, 4, 8, 128, 1024],
                             BF16).ap()
    wbf_w2 = nc.dram_tensor("wbf_w2", [NE + NSH, 8, 128, D], BF16).ap()
    wbf_out = nc.dram_tensor("wbf_out", [32, 128, D], BF16).ap()
    st_xn = nc.dram_tensor("st_xn", [NT, 16, 128, TILE], BF16).ap()
    st_sz = nc.dram_tensor("st_sz", [NT, 32, 128, TILE], BF16).ap()
    st_yl = nc.dram_tensor("st_yl", [NT, NCH, 128, DI], BF16).ap()
    st_c = nc.dram_tensor("st_c", [NT, NCH, 128, 128], BF16).ap()
    st_dacs = nc.dram_tensor("st_dacs", [NT, 64, TILE], F32).ap()
    st_dalast = nc.dram_tensor("st_dalast", [NT, NCH, 64], F32).ap()
    st_cmb = nc.dram_tensor("st_cmb", [NT, NE, TILE], F32).ap()
    st_s = nc.dram_tensor("st_s", [NT, 16, 128, TILE], BF16).ap()
    cc_in = nc.dram_tensor("cc_in", [128, 4160], BF16).ap()
    cc_out = nc.dram_tensor("cc_out", [n_cores * 128, 4160], BF16,
                            addr_space="Shared" if n_cores > 4
                            else "Local").ap()

    with ExitStack() as ctx:
        MASTER_N = 48896
        master = ctx.enter_context(
            nc.sbuf_tensor("master", [128, MASTER_N], F32))[:, :]

        def sbt(name, shape, dt):
            return ctx.enter_context(nc.sbuf_tensor(name, shape, dt))

        eye_bf = sbt("eye_bf_s", [128, 128], BF16)
        eye_f = sbt("eye_f_s", [128, 128], F32)
        m_up = sbt("m_up_s", [128, 128], F32)
        ones_c = sbt("ones_c", [128, 8], F32)
        ones_r = sbt("ones_r", [1, 512], F32)
        conv_w_s = sbt("conv_w_s", [128, 136], F32)
        conv_b_s = sbt("conv_b_s", [128, 34], F32)
        a_s = sbt("a_s", [64, 1], F32)
        dtb_s = sbt("dtb_s", [64, 1], F32)
        dbc_s = sbt("dbc_s", [128, 64], F32)
        nw_s = sbt("nw_s", [128, 32], F32)
        wa_f = sbt("wa_f", [128, 256], F32)
        wa_hi = sbt("wa_hi", [128, 256], BF16)
        wa_lo = sbt("wa_lo", [128, 256], BF16)
        rgb_s = sbt("rgb_s", [128, NE * 8], F32)
        rb1_s = sbt("rb1_s", [128, NE * 8], F32)
        sgb_s = sbt("sgb_s", [128, NSH * 8], F32)
        sb1_s = sbt("sb1_s", [128, NSH * 8], F32)
        selT_s = sbt("selT_s", [8, 8], F32)
        maskT_s = sbt("maskT_s", [64, 8], F32)
        cst_s = sbt("cst_s", [128, 8], F32)
        ln_cols = sbt("ln_cols", [128, 8], F32)
        smallrow = sbt("smallrow", [1, 128], F32)
        tail = sbt("tail", [128, 34 * 3], BF16)
        prevs_t = sbt("prevs_t", [128, DI], BF16)
        ld_col = sbt("ld_col", [64, 1], F32)
        zrow = sbt("zrow", [128, 64], BF16)
        halo_fm = sbt("halo_fm", [128, 16 * 4], BF16)

        psum = [ctx.enter_context(nc.psum_tensor(f"ps{i}", [128, 512], F32))
                for i in range(8)]

        block = ctx.enter_context(nc.Block())
        p = Prog(nc, ctx, block)
        gb = p.gb
        pbank = [gb(f"ps{i}") for i in range(8)]
        ar = Arena(master, MASTER_N)

        ar.push()
        x_tm = [ar.f32(D) for _ in range(2)]
        sq_scr = ar.f32(D)
        xn_tm_hi = ar.bf16(D)
        xn_tm_lo = ar.bf16(D)
        xn_fm_hi = ar.bf16(16 * TILE)
        xn_fm_lo = ar.bf16(16 * TILE)
        w_win = [ar.bf16(16 * WINC) for _ in range(2)]
        sz_ring = [ar.bf16(512) for _ in range(4)]
        xbcp = [ar.bf16(520) for _ in range(4)]
        conv_scr = [ar.f32(512) for _ in range(2)]
        x_fm = ar.bf16(32 * 512)
        b_fm = ar.bf16(512)
        c_fm = ar.bf16(512)
        dt_fm = ar.f32(512, parts=64)
        dt_scr = ar.f32(512, parts=64)
        dacs_fm = ar.f32(512, parts=64)
        zeros64 = ar.f32(128, parts=64)
        darg_fm = ar.f32(128, parts=64)
        dacs_tm = ar.f32(64)
        dt_tm = ar.f32(64)
        dec_tm = ar.f32(64)
        cdec_bc = ar.f32(64)
        b_tm = ar.bf16(128)
        bc_sb = ar.f32(128)
        rowbuf = [ar.f32(1024, parts=1) for _ in range(2)]
        x_tm_c = ar.bf16(DI)
        dtx = ar.bf16(DI)
        bce_sb = [ar.f32(512) for _ in range(2)]
        csc_sb = [ar.bf16(512) for _ in range(2)]
        lt_sb = [ar.f32(128) for _ in range(4)]
        stx_sb = [ar.bf16(128) for _ in range(4)]
        scan_scr = ar.f32(512)
        ylc = [ar.bf16(DI)]
        ylc = [ylc[0], ylc[0]]
        aff_tm = ar.f32(16)
        cmb_scr = ar.f32(48)
        cmb_fm = ar.f32(512, parts=16)
        tcw = [ar.bf16(1024) for _ in range(2)]
        tco_f = ar.f32(512)
        tco_b = ar.bf16(512)
        a_top = ar.off
        ar.pop()
        print(f"[arena] phase A: {a_top * 4 / 1024:.1f} KiB/part")

        ar.push()
        gthr = [ar.bf16(DI) for _ in range(2)]
        fold_scr = ar.f32(DI)
        ex_scr = ar.f32(512)
        ldg = ar.f32(64, parts=8)
        warg = ar.f32(8, parts=64)
        yl_b = [ar.bf16(DI) for _ in range(2)]
        sz_b = ar.bf16(32 * 512)
        yn_b = ar.bf16(32 * 512)
        yg_sb = ar.f32(DI)
        ygsq = ar.f32(128)
        rstd_r = ar.f32(128, parts=1)
        wo_win = [ar.bf16(32 * 256) for _ in range(2)]
        s_ev = [ar.bf16(512) for _ in range(4)]
        c_b = [ar.bf16(128) for _ in range(2)]
        rowb1 = [ar.f32(1024, parts=1) for _ in range(2)]
        srow64 = ar.f32(64, parts=1)
        cdec_b1 = ar.f32(64)
        bce_b1 = [ar.f32(512) for _ in range(2)]
        csc_b1 = [ar.bf16(512) for _ in range(2)]
        b1_top = ar.off
        ar.pop()
        print(f"[arena] phase B1: {b1_top * 4 / 1024:.1f} KiB/part")

        ar.push()
        xn_b2 = ar.bf16(16 * TILE)
        ffn_acc = ar.f32(16 * TILE)
        h_b2 = ar.bf16(8 * TILE)
        w2full = ar.bf16(8 * D)
        wgw1 = [ar.bf16(1024) for _ in range(4)]
        s_ld = [ar.bf16(16 * 128) for _ in range(2)]
        s_tm = ar.f32(D)
        g_tm = ar.f32(D)
        x_ld = [ar.f32(D) for _ in range(2)]
        out_b = [ar.f32(D) for _ in range(2)]
        gw_scr2 = [ar.f32(512) for _ in range(2)]
        w1_scr2 = [ar.f32(512) for _ in range(2)]
        cmb_row = ar.f32(512, parts=1)
        cmb_bc = ar.f32(512)
        rb2_row = ar.f32(512, parts=1)
        b2_top = ar.off
        ar.pop()
        print(f"[arena] phase B2: {b2_top * 4 / 1024:.1f} KiB/part")
        inc_bf = prevs_t[:, :]

        p.dma("sp", "pre", [
            (eye_bf[:, :], eye_bf_in), (eye_f[:, :], eye_f_in),
            (m_up[:, :], m_up_in), (ones_c[:, :], ones_in),
            (ones_r[:1, :], onesr_in), (conv_w_s[:, :], conv_w_rs),
            (conv_b_s[:, :], conv_b_rs), (a_s[:, :], a_col),
            (dtb_s[:, :], dtb_col), (dbc_s[:, :], d_bc),
            (nw_s[:, :], norm_w_rs), (wa_f[:, :], wa_rs),
            (rgb_s[:, :], rgb_rs), (rb1_s[:, :], rb1_rs),
            (sgb_s[:, :], sgb_rs), (sb1_s[:, :], sb1_rs),
            (selT_s[:, :], selT_in), (maskT_s[:, :], maskT_in),
            (cst_s[:, :], cst_in),
        ], writes=[gb("globals")])
        p.stage("dve", lambda E: E.tensor_copy(out=wa_hi[:, :], in_=wa_f[:, :]),
                reads=[gb("globals")], writes=[gb("wa_hi")])
        p.stage("dve", lambda E: E.tensor_tensor(
            out=wa_lo[:, :], in0=wa_f[:, :], in1=wa_hi[:, :],
            op=ALU.subtract),
            reads=[gb("globals"), gb("wa_hi")], writes=[gb("wa_lo")])
        p.stage("pool", lambda E: (
            E.memset(prevs_t[:, :], 0.0), E.memset(ld_col[:, :], 0.0),
            E.memset(zeros64, 0.0), E.memset(tail[:, :], 0.0),
            E.memset(zrow[:, :], 0.0))[-1],
            writes=[gb("prevs"), gb("ld"), gb("z64"), gb("tail"),
                    gb("ldbf")])

        tc_jobs = [("wo", kc, cc4) for kc in range(32) for cc4 in range(4)]
        for e in range(NE + NSH):
            ei = e if e < NE else e - NE
            gs = rg_in if e < NE else sg_in
            w1s = rw1_in if e < NE else sw1_in
            w2s = rw2_in if e < NE else sw2_in
            for q in range(4):
                for kp in range(8):
                    tc_jobs.append(("gw1", e, q, kp, gs, w1s, ei))
            for kc in range(8):
                for half in range(2):
                    tc_jobs.append(("w2", e, kc, half, w2s, ei))
        tc_i = [0]

        def emit_transcode(n):
            for _ in range(n):
                if tc_i[0] >= len(tc_jobs):
                    return
                job = tc_jobs[tc_i[0]]
                tc_i[0] += 1
                sl = tc_i[0] % 2
                bb = gb(f"tcw{sl}")
                if job[0] == "wo":
                    _, kc, cc4 = job
                    p.dma("pool", "tcof",
                          [(tco_f, w_out[kc * 128:(kc + 1) * 128,
                                         cc4 * 512:(cc4 + 1) * 512])],
                          writes=[gb("tco_f")])
                    p.stage("dve", lambda E, kc=kc: E.tensor_scalar(
                        out=tco_b, in0=tco_f, scalar1=nw_s[:, kc:kc + 1],
                        scalar2=None, op0=ALU.mult),
                        reads=[gb("tco_f"), gb("globals")],
                        writes=[gb("tco_b")])
                    p.dma("pool", "tcob",
                          [(wbf_out[kc, :, cc4 * 512:(cc4 + 1) * 512], tco_b)],
                          reads=[gb("tco_b")], writes=[gb("D_wo")])
                elif job[0] == "gw1":
                    _, e, q, kp, gs, w1s, ei = job
                    prs = []
                    for kc2 in range(2):
                        kc = kp * 2 + kc2
                        prs.append((tcw[sl][:, kc2 * 512:kc2 * 512 + 256],
                                    gs[ei, kc * 128:(kc + 1) * 128,
                                       q * 256:(q + 1) * 256]))
                        prs.append((tcw[sl][:, kc2 * 512 + 256:
                                            kc2 * 512 + 512],
                                    w1s[ei, kc * 128:(kc + 1) * 128,
                                        q * 256:(q + 1) * 256]))
                    p.dma("pool", f"tc{sl}", prs, writes=[bb])
                    p.dma("pool", f"tc{sl}",
                          [(wbf_gw1[e, q, kp, :, :], tcw[sl][:, :])],
                          reads=[bb], writes=[gb("D_gw1")])
                else:
                    _, e, kc, half, w2s, ei = job
                    p.dma("pool", f"tc{sl}",
                          [(tcw[sl][:, :],
                            w2s[ei, kc * 128:(kc + 1) * 128,
                                half * 1024:(half + 1) * 1024])],
                          writes=[bb])
                    p.dma("pool", f"tc{sl}",
                          [(wbf_w2[e, kc, :, half * 1024:(half + 1) * 1024],
                            tcw[sl][:, :])],
                          reads=[bb], writes=[gb("D_w2")])

        def ln_tokens(xb, xbufr, r, sl_hi, sl_lo):
            lc = gb("ln_cols")
            p.stage("dve", lambda E: E.tensor_reduce(
                out=ln_cols[:r, 0:1], in_=xb[:r, :], axis=AXL.X, op=ALU.add),
                reads=[xbufr], writes=[lc])
            p.stage("act", lambda E: E.activation(
                out=sq_scr[:r, :], in_=xb[:r, :], func=AF.Square,
                accum_out=ln_cols[:r, 1:2]),
                reads=[xbufr], writes=[gb("sq_scr"), lc])
            p.stage("dve", lambda E: E.tensor_scalar(
                out=ln_cols[:r, 2:3], in0=ln_cols[:r, 0:1],
                scalar1=cst_s[:r, 4:5], scalar2=None, op0=ALU.mult),
                reads=[lc, gb("globals")], writes=[lc])
            p.stage("dve", lambda E: E.tensor_tensor(
                out=ln_cols[:r, 3:4], in0=ln_cols[:r, 2:3],
                in1=ln_cols[:r, 2:3], op=ALU.mult),
                reads=[lc], writes=[lc])
            p.stage("dve", lambda E: E.scalar_tensor_tensor(
                out=ln_cols[:r, 3:4], in0=ln_cols[:r, 1:2],
                scalar=cst_s[:r, 4:5],
                in1=ln_cols[:r, 3:4], op0=ALU.mult, op1=ALU.subtract),
                reads=[lc, gb("globals")], writes=[lc])
            p.stage("act", lambda E: E.activation(
                out=ln_cols[:r, 4:5], in_=ln_cols[:r, 3:4], func=AF.Ln,
                bias=cst_s[:r, 0:1]),
                reads=[lc, gb("globals")], writes=[lc])
            p.stage("act", lambda E: E.activation(
                out=ln_cols[:r, 4:5], in_=ln_cols[:r, 4:5], func=AF.Exp,
                scale=cst_s[:r, 3:4]),
                reads=[lc, gb("globals")], writes=[lc])
            p.stage("dve", lambda E: E.tensor_tensor(
                out=ln_cols[:r, 5:6], in0=ln_cols[:r, 2:3],
                in1=ln_cols[:r, 4:5], op=ALU.mult),
                reads=[lc], writes=[lc])
            p.stage("dve", lambda E: E.tensor_scalar(
                out=sq_scr[:r, :], in0=xb[:r, :], scalar1=ln_cols[:r, 4:5],
                scalar2=ln_cols[:r, 5:6], op0=ALU.mult, op1=ALU.subtract),
                reads=[xbufr, lc], writes=[gb("sq_scr")])
            p.stage("dve", lambda E: E.tensor_copy(out=sl_hi, in_=sq_scr[:r, :]),
                    reads=[gb("sq_scr")], writes=[gb("xn_hl")])
            p.stage("dve", lambda E: E.tensor_tensor(
                out=sl_lo, in0=sq_scr[:r, :], in1=sl_hi, op=ALU.subtract),
                reads=[gb("sq_scr"), gb("xn_hl")], writes=[gb("xn_hl")])

        p.dma("sp", "xh", [(x_tm[0][:4, :], xhalo)], writes=[gb("x_tm0")])
        ln_tokens(x_tm[0], gb("x_tm0"), 4, xn_tm_hi[:4, :], xn_tm_lo[:4, :])
        for kc in range(16):
            p.stage("pe", lambda E, kc=kc: E.transpose(
                psum[6][:, :].bitcast(BF16)[:, 0:4],
                xn_tm_hi[:4, kc * 128:(kc + 1) * 128], eye_bf[:4, :4]),
                reads=[gb("xn_hl"), gb("globals")], writes=[pbank[6]])
            p.stage("dve", lambda E, kc=kc: E.tensor_copy(
                out=halo_fm[:, kc * 4:(kc + 1) * 4],
                in_=psum[6][:, :].bitcast(BF16)[:, 0:4]),
                reads=[pbank[6]], writes=[gb("halo_fm")])
        first_conv_w = DI // WINC
        for w in range(first_conv_w, NWIN):
            wcols = min(WINC, PROJ - w * WINC)
            sl = w % 2
            wb = gb(f"w_win{sl}")
            p.dma("pool", f"ww{sl}",
                  [(w_win[sl][:, kc * WINC:kc * WINC + wcols],
                    w_in[kc * 128:(kc + 1) * 128,
                         w * WINC:w * WINC + wcols]) for kc in range(16)],
                  writes=[wb])
            for mc in range((wcols + 127) // 128):
                mcw = min(128, wcols - mc * 128)
                gf = w * WINC + mc * 128
                if gf >= DI + CONV or gf < DI:
                    continue
                bi = 4 + mc % 2

                def hbody(E, sl=sl, mc=mc, mcw=mcw, bi=bi):
                    last = None
                    for kc in range(16):
                        last = E.matmul(
                            psum[bi][:mcw, 0:4],
                            w_win[sl][:, kc * WINC + mc * 128:
                                      kc * WINC + mc * 128 + mcw],
                            halo_fm[:, kc * 4:(kc + 1) * 4],
                            start=(kc == 0), stop=(kc == 15))
                    return last
                p.stage("pe", hbody, reads=[wb, gb("halo_fm")],
                        writes=[pbank[bi]])
                ci = (gf - DI) // 128
                p.stage("dve", lambda E, ci=ci, mcw=mcw, bi=bi: E.tensor_copy(
                    out=tail[:mcw, ci * 3:ci * 3 + 3],
                    in_=psum[bi][:mcw, 0:3]),
                    reads=[pbank[bi]], writes=[gb("tail")])
        emit_transcode(10)

        for t in range(NT):
            for tc in range(NCH):
                sl = tc % 2
                xb = gb(f"x_tm{sl}")
                p.dma("sp", f"xa{sl}",
                      [(x_tm[sl], x_in[t * TILE + tc * 128:
                                       t * TILE + tc * 128 + 128, :])],
                      writes=[xb])
                ln_tokens(x_tm[sl], xb, 128, xn_tm_hi[:, :], xn_tm_lo[:, :])
                for half, (src, dst, tb) in enumerate(
                        [(xn_tm_hi, xn_fm_hi, gb("xn_fm_hi")),
                         (xn_tm_lo, xn_fm_lo, gb("xn_fm_lo"))]):
                    for g8 in range(2):
                        bi = g8 % 2

                        def tb_body(E, src=src, g8=g8, bi=bi):
                            last = None
                            for j in range(8):
                                kc = g8 * 8 + j
                                last = E.transpose(
                                    psum[bi][:, :].bitcast(BF16)[
                                        :, j * 128:(j + 1) * 128],
                                    src[:, kc * 128:(kc + 1) * 128],
                                    eye_bf[:, :])
                            return last
                        p.stage("pe", tb_body,
                                reads=[gb("xn_hl"), gb("globals")],
                                writes=[pbank[bi]])

                        def tb_ev(E, dst=dst, g8=g8, tc=tc, bi=bi):
                            d3 = dst.rearrange("p (k c) -> p k c", c=TILE)[
                                :, g8 * 8:(g8 + 1) * 8, tc * 128:tc * 128 + 128]
                            return E.tensor_copy(
                                out=d3,
                                in_=psum[bi][:, :].bitcast(BF16)[:, 0:1024])
                        p.stage("dve", tb_ev, reads=[pbank[bi]], writes=[tb])
            def rbody(E):
                last = None
                n = 0
                for kc in range(16):
                    for (lh, rh) in ((wa_hi, xn_fm_hi), (wa_lo, xn_fm_hi),
                                     (wa_hi, xn_fm_lo)):
                        n += 1
                        last = E.matmul(
                            psum[7][:16, 0:512], lh[:, kc * 16:(kc + 1) * 16],
                            rh[:, kc * TILE:(kc + 1) * TILE],
                            start=(n == 1), stop=(n == 48))
                return last
            p.stage("pe", rbody,
                    reads=[gb("wa_hi"), gb("wa_lo"), gb("xn_fm_hi"),
                           gb("xn_fm_lo")],
                    writes=[pbank[7]])
            p.stage("act", lambda E: E.activation(
                out=cmb_fm, in_=psum[7][:16, 0:512], func=AF.Sigmoid),
                reads=[pbank[7]], writes=[gb("cmb_fm")])
            for tc in range(NCH):
                p.stage("pe", lambda E, tc=tc: E.transpose(
                    psum[6][:, 0:16],
                    cmb_fm[:, tc * 128:(tc + 1) * 128], eye_f[:16, :16]),
                    reads=[gb("cmb_fm"), gb("globals")], writes=[pbank[6]])
                p.stage("dve", lambda E: E.tensor_copy(
                    out=aff_tm, in_=psum[6][:, 0:16]),
                    reads=[pbank[6]], writes=[gb("aff_tm")])
                cs = cmb_scr
                p.stage("dve", lambda E: E.tensor_reduce(
                    out=cs[:, 0:1], in_=aff_tm, axis=AXL.X, op=ALU.max),
                    reads=[gb("aff_tm")], writes=[gb("cmb_scr")])
                p.stage("dve", lambda E: E.tensor_scalar(
                    out=cs[:, 16:32], in0=aff_tm, scalar1=cs[:, 0:1],
                    scalar2=None, op0=ALU.is_ge),
                    reads=[gb("aff_tm"), gb("cmb_scr")],
                    writes=[gb("cmb_scr")])
                p.stage("dve", lambda E: E.scalar_tensor_tensor(
                    out=cs[:, 32:48], in0=cs[:, 16:32],
                    scalar=cst_s[:, 6:7],
                    in1=aff_tm, op0=ALU.mult, op1=ALU.add),
                    reads=[gb("aff_tm"), gb("cmb_scr"), gb("globals")],
                    writes=[gb("cmb_scr")])
                p.stage("dve", lambda E: E.tensor_reduce(
                    out=cs[:, 1:2], in_=cs[:, 32:48], axis=AXL.X, op=ALU.max),
                    reads=[gb("cmb_scr")], writes=[gb("cmb_scr")])
                p.stage("dve", lambda E: E.tensor_tensor(
                    out=cs[:, 2:3], in0=cs[:, 0:1], in1=cs[:, 1:2],
                    op=ALU.add),
                    reads=[gb("cmb_scr")], writes=[gb("cmb_scr")])
                p.stage("dve", lambda E: E.reciprocal(
                    out=cs[:, 3:4], in_=cs[:, 2:3]),
                    reads=[gb("cmb_scr")], writes=[gb("cmb_scr")])
                p.stage("dve", lambda E: E.tensor_scalar(
                    out=cs[:, 16:32], in0=aff_tm, scalar1=cs[:, 1:2],
                    scalar2=None, op0=ALU.is_ge),
                    reads=[gb("aff_tm"), gb("cmb_scr")],
                    writes=[gb("cmb_scr")])
                p.stage("dve", lambda E: E.tensor_tensor(
                    out=cs[:, 32:48], in0=aff_tm, in1=cs[:, 16:32],
                    op=ALU.mult),
                    reads=[gb("aff_tm"), gb("cmb_scr")],
                    writes=[gb("cmb_scr")])
                p.stage("dve", lambda E: E.tensor_scalar(
                    out=cs[:, 32:48], in0=cs[:, 32:48], scalar1=cs[:, 3:4],
                    scalar2=None, op0=ALU.mult),
                    reads=[gb("cmb_scr")], writes=[gb("cmb_scr")])
                p.stage("pe", lambda E, tc=tc: E.transpose(
                    psum[6][:16, 0:128], cs[:, 32:48], eye_f[:, :]),
                    reads=[gb("cmb_scr"), gb("globals")], writes=[pbank[6]])
                p.stage("dve", lambda E, tc=tc: E.tensor_copy(
                    out=cmb_fm[:, tc * 128:(tc + 1) * 128],
                    in_=psum[6][:16, 0:128]),
                    reads=[pbank[6]], writes=[gb("cmb_fm")])
            p.dma("act", "cmbst", [(st_cmb[t, :, :], cmb_fm)],
                  reads=[gb("cmb_fm")], writes=[gb("D_cmb")])

            xbcp_i = [0]
            for w in range(NWIN):
                wcols = min(WINC, PROJ - w * WINC)
                sl = w % 2
                wb = gb(f"w_win{sl}")
                p.dma("pool", f"ww{sl}",
                      [(w_win[sl][:, kc * WINC:kc * WINC + wcols],
                        w_in[kc * 128:(kc + 1) * 128,
                             w * WINC:w * WINC + wcols]) for kc in range(16)],
                      writes=[wb])
                for mc in range((wcols + 127) // 128):
                    mcw = min(128, wcols - mc * 128)
                    gf = w * WINC + mc * 128
                    bi = (w % 4)

                    def wbody(E, sl=sl, mc=mc, mcw=mcw, bi=bi):
                        last = None
                        for kc in range(16):
                            last = E.matmul(
                                psum[bi][:mcw, 0:512],
                                w_win[sl][:, kc * WINC + mc * 128:
                                          kc * WINC + mc * 128 + mcw],
                                xn_fm_hi[:, kc * TILE:(kc + 1) * TILE],
                                start=(kc == 0), stop=(kc == 15))
                        return last
                    p.stage("pe", wbody, reads=[wb, gb("xn_fm_hi")],
                            writes=[pbank[bi]])
                    if gf < DI:
                        q = gf // 128
                        srs = q % 4
                        p.stage("act", lambda E, srs=srs, bi=bi: E.activation(
                            out=sz_ring[srs], in_=psum[bi][:, 0:512],
                            func=AF.Silu),
                            reads=[pbank[bi]], writes=[gb(f"sz{srs}")])
                        p.dma("act", f"szst{srs}",
                              [(st_sz[t, q, :, :], sz_ring[srs])],
                              reads=[gb(f"sz{srs}")], writes=[gb("D_sz")])
                    elif gf < DI + CONV:
                        ci = (gf - DI) // 128
                        xs = xbcp_i[0] % 4
                        xbcp_i[0] += 1
                        xb2 = gb(f"xbcp{xs}")
                        p.stage("dve", lambda E, xs=xs, bi=bi: E.tensor_copy(
                            out=xbcp[xs][:, 3:515], in_=psum[bi][:, 0:512]),
                            reads=[pbank[bi]], writes=[xb2])
                        p.stage("dve", lambda E, xs=xs, ci=ci: E.tensor_copy(
                            out=xbcp[xs][:, 0:3],
                            in_=tail[:, ci * 3:ci * 3 + 3]),
                            reads=[gb("tail")], writes=[xb2])
                        cvs = ci % 2
                        cb = gb(f"cscr{cvs}")
                        p.stage("dve", lambda E, xs=xs, ci=ci, cvs=cvs:
                                E.tensor_scalar(
                                    out=conv_scr[cvs], in0=xbcp[xs][:, 0:512],
                                    scalar1=conv_w_s[:, ci * 4:ci * 4 + 1],
                                    scalar2=None, op0=ALU.mult),
                                reads=[xb2, gb("globals")], writes=[cb])
                        for k in range(1, 4):
                            p.stage("dve",
                                    lambda E, xs=xs, ci=ci, cvs=cvs, k=k:
                                    E.scalar_tensor_tensor(
                                        out=conv_scr[cvs],
                                        in0=xbcp[xs][:, k:k + 512],
                                        scalar=conv_w_s[:, ci * 4 + k:
                                                        ci * 4 + k + 1],
                                        in1=conv_scr[cvs],
                                        op0=ALU.mult, op1=ALU.add),
                                    reads=[xb2, cb, gb("globals")],
                                    writes=[cb])
                        p.stage("dve", lambda E, xs=xs, ci=ci: E.tensor_copy(
                            out=tail[:, ci * 3:ci * 3 + 3],
                            in_=xbcp[xs][:, 512:515]),
                            reads=[xb2], writes=[gb("tail")])
                        if ci < 32:
                            dst_ap = x_fm[:, ci * 512:(ci + 1) * 512]
                            dstb = gb("x_fm")
                        elif ci == 32:
                            dst_ap = b_fm[:, :]
                            dstb = gb("b_fm")
                        else:
                            dst_ap = c_fm[:, :]
                            dstb = gb("c_fm")
                        p.stage("act", lambda E, cvs=cvs, ci=ci,
                                dst_ap=dst_ap: E.activation(
                                    out=dst_ap, in_=conv_scr[cvs],
                                    func=AF.Silu,
                                    bias=conv_b_s[:, ci:ci + 1]),
                                reads=[cb, gb("globals")], writes=[dstb])
                    else:
                        p.stage("act", lambda E, bi=bi: (
                            E.activation(out=dt_scr, in_=psum[bi][:64, 0:512],
                                         func=AF.Abs, bias=dtb_s[:, :]),
                            E.activation(out=dt_fm, in_=psum[bi][:64, 0:512],
                                         func=AF.Relu, bias=dtb_s[:, :]))[-1],
                            reads=[pbank[bi], gb("globals")],
                            writes=[gb("dt_scr"), gb("dt_fm")])
                        p.stage("act", lambda E: E.activation(
                            out=dt_scr, in_=dt_scr, func=AF.Exp,
                            scale=cst_s[:64, 2:3]),
                            reads=[gb("dt_scr"), gb("globals")],
                            writes=[gb("dt_scr")])
                        p.stage("act", lambda E: E.activation(
                            out=dt_scr, in_=dt_scr, func=AF.Ln,
                            bias=cst_s[:64, 1:2]),
                            reads=[gb("dt_scr"), gb("globals")],
                            writes=[gb("dt_scr")])
                        p.stage("dve", lambda E: E.tensor_tensor(
                            out=dt_fm, in0=dt_fm, in1=dt_scr, op=ALU.add),
                            reads=[gb("dt_scr"), gb("dt_fm")],
                            writes=[gb("dt_fm")])
            p.stage("dve", lambda E: E.tensor_scalar(
                out=dacs_fm, in0=dt_fm, scalar1=a_s[:, :], scalar2=None,
                op0=ALU.mult),
                reads=[gb("dt_fm"), gb("globals")], writes=[gb("dacs_fm")])

            def csbody(E):
                last = None
                for c in range(NCH):
                    last = E.tensor_tensor_scan(
                        out=dacs_fm[:, c * 128:(c + 1) * 128],
                        data0=dacs_fm[:, c * 128:(c + 1) * 128],
                        data1=zeros64, initial=0.0, op0=ALU.add, op1=ALU.add)
                return last
            p.stage("dve", csbody, reads=[gb("z64"), gb("dacs_fm")],
                    writes=[gb("dacs_fm")])
            p.dma("act", "dacsst", [(st_dacs[t, :, :], dacs_fm)],
                  reads=[gb("dacs_fm")], writes=[gb("D_dacs")])
            for c in range(NCH):
                p.stage("dve", lambda E, c=c: E.tensor_tensor(
                    out=ld_col[:, :], in0=ld_col[:, :],
                    in1=dacs_fm[:, c * 128 + 127:c * 128 + 128], op=ALU.add),
                    reads=[gb("dacs_fm")], writes=[gb("ld")])

            for c in range(NCH):
                for g8 in range(4):
                    bi = g8 % 2

                    def xb_body(E, g8=g8, c=c, bi=bi):
                        last = None
                        for j in range(8):
                            q = g8 * 8 + j
                            last = E.transpose(
                                psum[bi][:, :].bitcast(BF16)[
                                    :, j * 128:(j + 1) * 128],
                                x_fm[:, q * 512 + c * 128:
                                     q * 512 + c * 128 + 128],
                                eye_bf[:, :])
                        return last
                    p.stage("pe", xb_body, reads=[gb("x_fm"), gb("globals")],
                            writes=[pbank[bi]])
                    p.stage("dve", lambda E, g8=g8, bi=bi: E.tensor_copy(
                        out=x_tm_c[:, g8 * 1024:(g8 + 1) * 1024],
                        in_=psum[bi][:, :].bitcast(BF16)[:, 0:1024]),
                        reads=[pbank[bi]], writes=[gb("x_tm_c")])
                p.stage("pe", lambda E, c=c: E.transpose(
                    psum[2][:, :].bitcast(BF16)[:, 0:128],
                    b_fm[:, c * 128:(c + 1) * 128], eye_bf[:, :]),
                    reads=[gb("b_fm"), gb("globals")], writes=[pbank[2]])
                p.stage("dve", lambda E: E.tensor_copy(
                    out=b_tm, in_=psum[2][:, :].bitcast(BF16)[:, 0:128]),
                    reads=[pbank[2]], writes=[gb("b_tm")])
                p.stage("dve", lambda E, c=c: E.tensor_scalar(
                    out=darg_fm, in0=dacs_fm[:, c * 128:(c + 1) * 128],
                    scalar1=cst_s[:64, 2:3],
                    scalar2=dacs_fm[:, c * 128 + 127:c * 128 + 128],
                    op0=ALU.mult, op1=ALU.add),
                    reads=[gb("dacs_fm"), gb("globals")],
                    writes=[gb("darg")])
                p.stage("pe", lambda E, c=c: (
                    E.transpose(psum[2][:, 0:64],
                                dacs_fm[:, c * 128:(c + 1) * 128],
                                eye_f[:64, :64]),
                    E.transpose(psum[2][:, 64:128],
                                dt_fm[:, c * 128:(c + 1) * 128],
                                eye_f[:64, :64]),
                    E.transpose(psum[2][:, 128:192], darg_fm,
                                eye_f[:64, :64]))[-1],
                    reads=[gb("dacs_fm"), gb("dt_fm"), gb("darg"),
                           gb("globals")],
                    writes=[pbank[2]])
                p.stage("dve", lambda E: E.tensor_copy(
                    out=dacs_tm, in_=psum[2][:, 0:64]),
                    reads=[pbank[2]], writes=[gb("dacs_tm")])
                p.stage("dve", lambda E: E.tensor_copy(
                    out=dt_tm, in_=psum[2][:, 64:128]),
                    reads=[pbank[2]], writes=[gb("dt_tm")])
                p.stage("act", lambda E: E.activation(
                    out=dec_tm, in_=psum[2][:, 128:192], func=AF.Exp),
                    reads=[pbank[2]], writes=[gb("dec_tm")])
                p.stage("pe", lambda E, c=c: E.transpose(
                    psum[3][:1, 0:64],
                    dacs_fm[:, c * 128 + 127:c * 128 + 128],
                    eye_f[:64, :64]),
                    reads=[gb("dacs_fm"), gb("globals")], writes=[pbank[3]])
                p.stage("dve", lambda E: E.tensor_copy(
                    out=smallrow[:1, 0:64], in_=psum[3][:1, 0:64]),
                    reads=[pbank[3]], writes=[gb("smallrow")])
                p.dma("act", "dalst",
                      [(st_dalast[t, c, :].unsqueeze(0), smallrow[:1, 0:64])],
                      reads=[gb("smallrow")], writes=[gb("D_dal")])
                p.stage("pe", lambda E: E.matmul(
                    psum[3][:, 64:128], ones_r[:1, 0:128],
                    smallrow[:1, 0:64], start=True, stop=True),
                    reads=[gb("smallrow"), gb("globals")], writes=[pbank[3]])
                p.stage("act", lambda E: E.activation(
                    out=cdec_bc, in_=psum[3][:, 64:128], func=AF.Exp),
                    reads=[pbank[3]], writes=[gb("cdec")])
                p.stage("pe", lambda E, c=c: E.matmul(
                    psum[6][:, 0:128], b_fm[:, c * 128:(c + 1) * 128],
                    c_fm[:, c * 128:(c + 1) * 128], start=True, stop=True),
                    reads=[gb("b_fm"), gb("c_fm")], writes=[pbank[6]])
                p.stage("dve", lambda E: E.tensor_copy(
                    out=bc_sb, in_=psum[6][:, 0:128]),
                    reads=[pbank[6]], writes=[gb("bc_sb")])
                p.dma("act", "cst",
                      [(st_c[t, c, :, :], c_fm[:, c * 128:(c + 1) * 128])],
                      reads=[gb("c_fm")], writes=[gb("D_c")])

                def dtx_b(E):
                    in1 = dt_tm.unsqueeze(2).to_broadcast([128, 64, 64])
                    return E.tensor_tensor(
                        out=dtx.rearrange("p (h d) -> p h d", d=64),
                        in0=x_tm_c.rearrange("p (h d) -> p h d", d=64),
                        in1=in1, op=ALU.mult)
                p.stage("dve", dtx_b, reads=[gb("x_tm_c"), gb("dt_tm")],
                        writes=[gb("dtx")])
                def rows_fetch(pd):
                    rs = pd % 2
                    p.dma("sp", f"rows{rs}",
                          [(rowbuf[rs][:1, (g - pd * 2) * 512:
                                       (g - pd * 2 + 1) * 512]
                            .rearrange("p (a b) -> p a b", b=128),
                            st_dacs[t, g * 4:(g + 1) * 4,
                                    c * 128:(c + 1) * 128].unsqueeze(0))
                           for g in range(pd * 2, pd * 2 + 2)],
                          writes=[gb(f"rows{rs}")], reads=[gb("D_dacs")])
                rows_fetch(0)
                rows_fetch(1)
                for g in range(16):
                    if g % 2 == 0 and g >= 4:
                        rows_fetch(g // 2)
                    rs = (g // 2) % 2
                    rb = gb(f"rows{rs}")
                    ab = 2 + (g % 2)
                    yb = 4 + (g % 2)
                    bcb = gb(f"bce{g % 2}")
                    cscb = gb(f"csc{g % 2}")
                    p.stage("pe", lambda E, g=g, ab=ab, rs=rs: E.matmul(
                        psum[ab][:, 0:512], ones_r[:1, 0:128],
                        rowbuf[rs][:1, (g % 2) * 512:(g % 2 + 1) * 512],
                        start=True, stop=True),
                        reads=[rb, gb("globals")], writes=[pbank[ab]])
                    p.stage("act", lambda E, g=g, ab=ab: E.activation(
                        out=bce_sb[g % 2], in_=psum[ab][:, 0:512],
                        func=AF.Exp),
                        reads=[pbank[ab]], writes=[bcb])

                    def csc_f(E, g=g, c=c):
                        in0 = c_fm[:, c * 128:(c + 1) * 128].unsqueeze(1) \
                            .to_broadcast([128, 4, 128])
                        return E.tensor_tensor(
                            out=csc_sb[g % 2].rearrange(
                                "p (a b) -> p a b", b=128),
                            in0=in0, in1=bce_sb[g % 2].rearrange(
                                "p (a b) -> p a b", b=128),
                            op=ALU.mult)
                    p.stage("dve", csc_f, reads=[gb("c_fm"), bcb],
                            writes=[cscb])
                    for hh in range(4):
                        h = g * 4 + hh
                        ls = hh % 2
                        lb = gb(f"lt{ls}")
                        sb_ = gb(f"stx{ls}")
                        if hh == 0:
                            def lt4(E, g=g, ab=ab):
                                last = None
                                for h2 in range(4):
                                    h3 = g * 4 + h2
                                    last = E.scalar_tensor_tensor(
                                        out=lt_sb[h2],
                                        in0=psum[ab][:, h2 * 128:
                                                     (h2 + 1) * 128],
                                        scalar=dacs_tm[:, h3:h3 + 1],
                                        in1=m_up[:, :],
                                        op0=ALU.subtract, op1=ALU.add)
                                return last
                            p.stage("dve", lt4,
                                    reads=[pbank[ab], gb("dacs_tm"),
                                           gb("globals")],
                                    writes=[gb("ltg")])
                        if hh == 0:
                            def exp4(E):
                                last = None
                                for h2 in range(4):
                                    last = E.activation(out=lt_sb[h2],
                                                        in_=lt_sb[h2],
                                                        func=AF.Exp)
                                return last
                            p.stage("act", exp4, reads=[gb("ltg")],
                                    writes=[gb("ltg")])
                        if hh == 0:
                            def st4(E):
                                last = None
                                for h2 in range(4):
                                    last = E.tensor_tensor(
                                        out=stx_sb[h2], in0=bc_sb,
                                        in1=lt_sb[h2], op=ALU.mult)
                                return last
                            p.stage("dve", st4,
                                    reads=[gb("ltg"), gb("bc_sb")],
                                    writes=[gb("stxg")])
                        if hh == 3:
                            def ymm8(E, g=g, yb=yb):
                                last = None
                                for h2 in range(4):
                                    h3 = g * 4 + h2
                                    E.matmul(
                                        psum[yb][:64, h2 * 128:(h2 + 1) * 128],
                                        dtx[:, h3 * 64:(h3 + 1) * 64],
                                        stx_sb[h2], start=True, stop=False)
                                    last = E.matmul(
                                        psum[yb][:64, h2 * 128:(h2 + 1) * 128],
                                        prevs_t[:, h3 * 64:(h3 + 1) * 64],
                                        csc_sb[g % 2][:, h2 * 128:
                                                      (h2 + 1) * 128],
                                        start=False, stop=True)
                                return last
                            p.stage("pe", ymm8,
                                    reads=[gb("dtx"), gb("stxg"), cscb,
                                           gb("prevs")],
                                    writes=[pbank[yb]])

                    def yev(E, g=g, c=c, yb=yb):
                        last = None
                        for hh in range(4):
                            h = g * 4 + hh
                            q = h // 2
                            r0 = (h % 2) * 64
                            last = E.scalar_tensor_tensor(
                                out=ylc[c % 2][r0:r0 + 64,
                                               q * 128:(q + 1) * 128],
                                in0=x_fm[r0:r0 + 64,
                                         q * 512 + c * 128:
                                         q * 512 + c * 128 + 128],
                                scalar=dbc_s[r0:r0 + 64, h:h + 1],
                                in1=psum[yb][:64, hh * 128:(hh + 1) * 128],
                                op0=ALU.mult, op1=ALU.add)
                        return last
                    p.stage("dve", yev,
                            reads=[gb("x_fm"), gb("globals"), pbank[yb]],
                            writes=[gb("ylc")])
                p.dma("act", f"ylst{c % 2}",
                      [(st_yl[t, c, :, :], ylc[c % 2])],
                      reads=[gb("ylc")], writes=[gb("D_yl")])

                def ddtx_b(E):
                    in1 = dec_tm.unsqueeze(2).to_broadcast([128, 64, 64])
                    return E.tensor_tensor(
                        out=dtx.rearrange("p (h d) -> p h d", d=64),
                        in0=dtx.rearrange("p (h d) -> p h d", d=64),
                        in1=in1, op=ALU.mult)
                p.stage("dve", ddtx_b, reads=[gb("dec_tm"), gb("dtx")],
                        writes=[gb("dtx")])
                for sg_ in range(8):
                    bi = sg_ % 2
                    p.stage("pe", lambda E, sg_=sg_, bi=bi: E.matmul(
                        psum[bi][:, 0:512], b_tm,
                        dtx[:, sg_ * 512:(sg_ + 1) * 512],
                        start=True, stop=True),
                        reads=[gb("b_tm"), gb("dtx")], writes=[pbank[bi]])

                    def scm(E, sg_=sg_):
                        in1 = cdec_bc[:, sg_ * 8:(sg_ + 1) * 8] \
                            .unsqueeze(2).to_broadcast([128, 8, 64])
                        return E.tensor_tensor(
                            out=scan_scr.rearrange("p (a b) -> p a b", b=64),
                            in0=prevs_t[:, sg_ * 512:(sg_ + 1) * 512]
                            .rearrange("p (a b) -> p a b", b=64),
                            in1=in1, op=ALU.mult)
                    p.stage("dve", scm, reads=[gb("prevs"), gb("cdec")],
                            writes=[gb("scan")])
                    p.stage("dve", lambda E, sg_=sg_, bi=bi: E.tensor_tensor(
                        out=prevs_t[:, sg_ * 512:(sg_ + 1) * 512],
                        in0=scan_scr, in1=psum[bi][:, 0:512], op=ALU.add),
                        reads=[gb("scan"), pbank[bi]], writes=[gb("prevs")])
            p.dma("act", "xnst",
                  [(st_xn[t, kc, :, :],
                    xn_fm_hi[:, kc * TILE:(kc + 1) * TILE])
                   for kc in range(16)],
                  reads=[gb("xn_fm_hi")], writes=[gb("D_xn")])
            emit_transcode(118 if t > 0 else 100)
        emit_transcode(10**9)

        p.stage("pe", lambda E: E.transpose(
            psum[0][:1, 0:64], ld_col[:, :], eye_f[:64, :64]),
            reads=[gb("ld"), gb("globals")], writes=[pbank[0]])
        p.stage("dve", lambda E: E.tensor_copy(out=zrow[0:1, :],
                                               in_=psum[0][:1, 0:64]),
                reads=[pbank[0]], writes=[gb("ldbf")])
        p.dma("act", "ccw", [(cc_in[:, 0:DI], prevs_t[:, :]),
                             (cc_in[:, 4096:4160], zrow[:, :])],
              reads=[gb("prevs"), gb("ldbf")])
        cc_sem = ctx.enter_context(nc.semaphore("cc_sem"))
        ccw = p.dma_sems["ccw"]

        @block.gpsimd
        def _(E):
            E.wait_ge(ccw[0], ccw[1])
            E.collective_compute(
                "AllGather", ALU.bypass,
                replica_groups=[list(range(n_cores))],
                ins=[cc_in[:, :]], outs=[cc_out[:, :]]).then_inc(cc_sem)
        cc_tok = (cc_sem, 1)
        p.barrier(extra=[cc_tok])
        p.dma("pool", "ldg",
              [(ldg[:n_cores, :].unsqueeze(1),
                cc_out[:, 4096:4160].rearrange("(j q) c -> j q c", q=128)[
                    :, 0:1, :])],
              writes=[gb("ldg")], deps=[cc_tok])
        p.stage("pe", lambda E: E.matmul(
            psum[0][:64, 0:8], ldg[:n_cores, :], selT_s[:n_cores, :],
            start=True, stop=True),
            reads=[gb("ldg"), gb("globals")], writes=[pbank[0]])
        p.stage("dve", lambda E: E.tensor_tensor(
            out=warg, in0=psum[0][:64, 0:8], in1=maskT_s[:, :], op=ALU.add),
            reads=[pbank[0], gb("globals")], writes=[gb("warg")])
        p.stage("act", lambda E: E.activation(
            out=warg, in_=warg, func=AF.Exp),
            reads=[gb("warg")], writes=[gb("warg")])
        p.stage("pool", lambda E: E.memset(fold_scr, 0.0),
                writes=[gb("fold")])
        for j in range(n_cores):
            gsl = j % 2
            gbj = gb(f"gthr{gsl}")
            p.dma("sp", f"gth{gsl}",
                  [(gthr[gsl], cc_out[j * 128:(j + 1) * 128, 0:DI])],
                  writes=[gbj], deps=[cc_tok])
            p.stage("pe", lambda E, j=j: E.transpose(
                psum[1][:1, 0:64], warg[:, j:j + 1], eye_f[:64, :64]),
                reads=[gb("warg"), gb("globals")], writes=[pbank[1]])
            p.stage("dve", lambda E: E.tensor_copy(
                out=smallrow[:1, 0:64], in_=psum[1][:1, 0:64]),
                reads=[pbank[1]], writes=[gb("smallrow")])
            for sl4 in range(8):
                bi = 2 + sl4 % 2

                def bcw(E, sl4=sl4, bi=bi):
                    rhs = smallrow[:1, sl4 * 8:(sl4 + 1) * 8].unsqueeze(2) \
                        .to_broadcast([1, 8, 64])
                    return E.matmul(psum[bi][:, 0:512], ones_r[:1, 0:128],
                                    rhs, start=True, stop=True)
                p.stage("pe", bcw, reads=[gb("smallrow"), gb("globals")],
                        writes=[pbank[bi]])
                p.stage("dve", lambda E, sl4=sl4, gsl=gsl, bi=bi:
                        E.tensor_tensor(
                            out=ex_scr,
                            in0=gthr[gsl][:, sl4 * 512:(sl4 + 1) * 512],
                            in1=psum[bi][:, 0:512], op=ALU.mult),
                        reads=[gbj, pbank[bi]], writes=[gb("exscr")])
                p.stage("pool", lambda E, sl4=sl4: E.tensor_tensor(
                    out=fold_scr[:, sl4 * 512:(sl4 + 1) * 512],
                    in0=fold_scr[:, sl4 * 512:(sl4 + 1) * 512],
                    in1=ex_scr, op=ALU.add),
                    reads=[gb("exscr"), gb("fold")], writes=[gb("fold")])
        p.stage("dve", lambda E: E.tensor_copy(out=inc_bf, in_=fold_scr),
                reads=[gb("fold")], writes=[gb("prevs")])

        for t in range(NT):
            szb = gb("szB")
            p.dma("sp", "szld",
                  [(sz_b[:, q * TILE:(q + 1) * TILE], st_sz[t, q, :, :])
                   for q in range(32)],
                  writes=[szb], reads=[gb("D_sz")])
            for c in range(NCH):
                ylb = gb(f"ylB{c % 2}")
                p.dma("sp", f"ylld{c % 2}",
                      [(yl_b[c % 2], st_yl[t, c, :, :])],
                      writes=[ylb], reads=[gb("D_yl")])
                cb2 = gb(f"cB{c % 2}")
                p.dma("sp", f"cld{c % 2}", [(c_b[c % 2], st_c[t, c, :, :])],
                      writes=[cb2], reads=[gb("D_c")])
                def rows_fetch_b(pd):
                    rs = pd % 2
                    p.dma("sp", f"rowsB{rs}",
                          [(rowb1[rs][:1, (g - pd * 2) * 512:
                                      (g - pd * 2 + 1) * 512]
                            .rearrange("p (a b) -> p a b", b=128),
                            st_dacs[t, g * 4:(g + 1) * 4,
                                    c * 128:(c + 1) * 128].unsqueeze(0))
                           for g in range(pd * 2, pd * 2 + 2)],
                          writes=[gb(f"rowsB{rs}")], reads=[gb("D_dacs")])
                rows_fetch_b(0)
                rows_fetch_b(1)
                p.dma("sp", "srow64",
                      [(srow64[:1, :],
                        st_dalast[t, c, :].unsqueeze(0))],
                      writes=[gb("srow64")], reads=[gb("D_dal")])
                p.stage("pe", lambda E: E.matmul(
                    psum[7][:, 0:64], ones_r[:1, 0:128], srow64[:1, :],
                    start=True, stop=True),
                    reads=[gb("srow64"), gb("globals")], writes=[pbank[7]])
                p.stage("act", lambda E: E.activation(
                    out=cdec_b1, in_=psum[7][:, 0:64], func=AF.Exp),
                    reads=[pbank[7]], writes=[gb("cdecB")])
                for g in range(16):
                    if g % 2 == 0 and g >= 4:
                        rows_fetch_b(g // 2)
                    rs = (g // 2) % 2
                    rb = gb(f"rowsB{rs}")
                    ab = 2 + g % 2
                    yb = 4 + g % 2
                    bcb = gb(f"bceB{g % 2}")
                    cscb = gb(f"cscB{g % 2}")
                    p.stage("pe", lambda E, g=g, ab=ab, rs=rs: E.matmul(
                        psum[ab][:, 0:512], ones_r[:1, 0:128],
                        rowb1[rs][:1, (g % 2) * 512:(g % 2 + 1) * 512],
                        start=True, stop=True),
                        reads=[rb, gb("globals")], writes=[pbank[ab]])
                    p.stage("act", lambda E, g=g, ab=ab: E.activation(
                        out=bce_b1[g % 2], in_=psum[ab][:, 0:512],
                        func=AF.Exp),
                        reads=[pbank[ab]], writes=[bcb])

                    def cscf(E, g=g, c=c):
                        in0 = c_b[c % 2].unsqueeze(1).to_broadcast(
                            [128, 4, 128])
                        return E.tensor_tensor(
                            out=csc_b1[g % 2].rearrange(
                                "p (a b) -> p a b", b=128),
                            in0=in0,
                            in1=bce_b1[g % 2].rearrange(
                                "p (a b) -> p a b", b=128),
                            op=ALU.mult)
                    p.stage("pool", cscf, reads=[cb2, bcb], writes=[cscb])

                    def ycm(E, g=g, yb=yb):
                        last = None
                        for hh in range(4):
                            h = g * 4 + hh
                            last = E.matmul(
                                psum[yb][:64, hh * 128:(hh + 1) * 128],
                                inc_bf[:, h * 64:(h + 1) * 64],
                                csc_b1[g % 2][:, hh * 128:(hh + 1) * 128],
                                start=True, stop=True)
                        return last
                    p.stage("pe", ycm, reads=[gb("prevs"), cscb],
                            writes=[pbank[yb]])

                    def gatef(E, g=g, c=c, yb=yb):
                        last = None
                        for hh in range(4):
                            h = g * 4 + hh
                            q = h // 2
                            r0 = (h % 2) * 64
                            last = E.tensor_tensor(
                                out=yg_sb[r0:r0 + 64, q * 128:(q + 1) * 128],
                                in0=yl_b[c % 2][r0:r0 + 64,
                                                q * 128:(q + 1) * 128],
                                in1=psum[yb][:64, hh * 128:(hh + 1) * 128],
                                op=ALU.add)
                        return last
                    p.stage("dve", gatef, reads=[ylb, pbank[yb]],
                            writes=[gb("yg")])
                for q in range(32):
                    p.stage("dve", lambda E, q=q, c=c: E.tensor_tensor(
                        out=yg_sb[:, q * 128:(q + 1) * 128],
                        in0=yg_sb[:, q * 128:(q + 1) * 128],
                        in1=sz_b[:, q * TILE + c * 128:
                                 q * TILE + c * 128 + 128],
                        op=ALU.mult),
                        reads=[gb("yg"), szb], writes=[gb("yg")])
                    p.stage("act", lambda E, q=q: E.activation(
                        out=ygsq, in_=yg_sb[:, q * 128:(q + 1) * 128],
                        func=AF.Square),
                        reads=[gb("yg")], writes=[gb("ygsq")])
                    p.stage("pe", lambda E, q=q: E.matmul(
                        psum[6][:1, 0:128], ones_c[:, 0:1], ygsq,
                        start=(q == 0), stop=(q == 31)),
                        reads=[gb("ygsq"), gb("globals")], writes=[pbank[6]])
                p.stage("act", lambda E: E.activation(
                    out=rstd_r[:1, :], in_=psum[6][:1, 0:128], func=AF.Ln,
                    bias=cst_s[0:1, 0:1], scale=cst_s[0:1, 5:6]),
                    reads=[pbank[6], gb("globals")], writes=[gb("rstd")])
                p.stage("act", lambda E: E.activation(
                    out=rstd_r[:1, :], in_=rstd_r[:1, :], func=AF.Exp,
                    scale=cst_s[0:1, 3:4]),
                    reads=[gb("rstd"), gb("globals")], writes=[gb("rstd")])
                p.stage("pe", lambda E: E.matmul(
                    psum[6][:, 0:128], ones_r[:1, 0:128], rstd_r[:1, :],
                    start=True, stop=True),
                    reads=[gb("rstd"), gb("globals")], writes=[pbank[6]])
                for q in range(32):
                    p.stage("dve", lambda E, q=q, c=c: E.tensor_tensor(
                        out=yn_b[:, q * TILE + c * 128:
                                 q * TILE + c * 128 + 128],
                        in0=yg_sb[:, q * 128:(q + 1) * 128],
                        in1=psum[6][:, 0:128], op=ALU.mult),
                        reads=[gb("yg"), pbank[6]], writes=[gb("ynB")])

                def incsc(E):
                    in1 = cdec_b1.unsqueeze(2).to_broadcast([128, 64, 64])
                    return E.tensor_tensor(
                        out=inc_bf.rearrange("p (h d) -> p h d", d=64),
                        in0=inc_bf.rearrange("p (h d) -> p h d", d=64),
                        in1=in1, op=ALU.mult)
                p.stage("dve", incsc, reads=[gb("cdecB"), gb("prevs")],
                        writes=[gb("prevs")])
            for w in range(8):
                sl = w % 2
                wob = gb(f"woB{sl}")
                p.dma("sp", f"wo{sl}",
                      [(wo_win[sl][:, kc * 256:(kc + 1) * 256],
                        wbf_out[kc, :, w * 256:(w + 1) * 256])
                       for kc in range(32)],
                      writes=[wob], reads=[gb("D_wo")])
                for mc in range(2):
                    bi = sl * 2 + mc

                    def wom(E, w=w, mc=mc, sl=sl, bi=bi):
                        last = None
                        for kc in range(32):
                            last = E.matmul(
                                psum[bi][:, 0:512],
                                wo_win[sl][:, kc * 256 + mc * 128:
                                           kc * 256 + mc * 128 + 128],
                                yn_b[:, kc * TILE:(kc + 1) * TILE],
                                start=(kc == 0), stop=(kc == 31))
                        return last
                    p.stage("pe", wom, reads=[wob, gb("ynB")],
                            writes=[pbank[bi]])
                    srs = (w * 2 + mc) % 4
                    p.stage("dve", lambda E, bi=bi, srs=srs: E.tensor_copy(
                        out=s_ev[srs], in_=psum[bi][:, 0:512]),
                        reads=[pbank[bi]], writes=[gb(f"sev{srs}")])
                    p.dma("act", f"sst{srs}",
                          [(st_s[t, w * 2 + mc, :, :], s_ev[srs])],
                          reads=[gb(f"sev{srs}")], writes=[gb("D_s")])

        p.barrier()
        for t in range(NT):
            xnb = gb("xnB2")
            p.dma("sp", "xnld",
                  [(xn_b2[:, kc * TILE:(kc + 1) * TILE], st_xn[t, kc, :, :])
                   for kc in range(16)],
                  writes=[xnb], reads=[gb("D_xn")])
            fb = gb("ffnB2")
            p.stage("dve", lambda E: E.tensor_copy(out=ffn_acc, in_=xn_b2),
                    reads=[xnb], writes=[fb])
            for e in range(NE + NSH):
                routed = e < NE
                if routed:
                    p.dma("sp", "cmbr",
                          [(cmb_row[:1, :], st_cmb[t, e, :].unsqueeze(0))],
                          writes=[gb("cmbrow")], reads=[gb("D_cmb")])
                    p.stage("pe", lambda E: E.matmul(
                        psum[0][:, 0:512], ones_r[:1, 0:128], cmb_row[:1, :],
                        start=True, stop=True),
                        reads=[gb("cmbrow"), gb("globals")],
                        writes=[pbank[0]])
                    p.stage("dve", lambda E: E.tensor_copy(
                        out=cmb_bc, in_=psum[0][:, 0:512]),
                        reads=[pbank[0]], writes=[gb("cmbbc")])
                for q in range(4):
                    qb = (q % 2) * 4
                    for kp in range(8):
                        sl = kp % 4
                        wsb = gb(f"wg{sl}")
                        p.dma("sp", f"wg{sl}",
                              [(wgw1[sl], wbf_gw1[e, q, kp, :, :])],
                              writes=[wsb], reads=[gb("D_gw1")])

                        def gwm(E, kp=kp, sl=sl, qb=qb):
                            last = None
                            for kc2 in range(2):
                                kc = kp * 2 + kc2
                                for mc in range(2):
                                    for off, b0 in ((0, 0), (256, 2)):
                                        last = E.matmul(
                                            psum[qb + b0 + mc][:, 0:512],
                                            wgw1[sl][:, kc2 * 512 + off +
                                                     mc * 128:
                                                     kc2 * 512 + off +
                                                     mc * 128 + 128],
                                            xn_b2[:, kc * TILE:
                                                  (kc + 1) * TILE],
                                            start=(kp == 0 and kc2 == 0),
                                            stop=(kp == 7 and kc2 == 1))
                            return last
                        p.stage("pe", gwm, reads=[wsb, xnb],
                                writes=[pbank[qb + i] for i in range(4)])
                    gbias = rgb_s if routed else sgb_s
                    bias1 = rb1_s if routed else sb1_s
                    bcol0 = (e if routed else e - NE) * 8 + q * 2

                    def ev_act(E, qb=qb, bcol0=bcol0, gbias=gbias,
                               bias1=bias1):
                        last = None
                        for mc in range(2):
                            E.activation(
                                out=gw_scr[:, mc * 512:(mc + 1) * 512]
                                if False else gw_scr2[mc],
                                in_=psum[qb + mc][:, 0:512], func=AF.Gelu,
                                bias=gbias[:, bcol0 + mc:bcol0 + mc + 1])
                            last = E.activation(
                                out=w1_scr2[mc],
                                in_=psum[qb + 2 + mc][:, 0:512],
                                func=AF.Identity,
                                bias=bias1[:, bcol0 + mc:bcol0 + mc + 1])
                        return last
                    p.stage("act", ev_act,
                            reads=[pbank[qb], pbank[qb + 1], pbank[qb + 2],
                                   pbank[qb + 3], gb("globals")],
                            writes=[gb("gwscr"), gb("w1scr")])

                    def ev_mul(E):
                        last = None
                        for mc in range(2):
                            last = E.tensor_tensor(
                                out=gw_scr2[mc], in0=gw_scr2[mc],
                                in1=w1_scr2[mc], op=ALU.mult)
                        return last
                    p.stage("dve", ev_mul, reads=[gb("gwscr"), gb("w1scr")],
                            writes=[gb("gwscr")])

                    def ev_h(E, q=q, routed=routed):
                        last = None
                        for mc in range(2):
                            hq = q * 2 + mc
                            if routed:
                                last = E.tensor_tensor(
                                    out=h_b2[:, hq * TILE:(hq + 1) * TILE],
                                    in0=gw_scr2[mc], in1=cmb_bc,
                                    op=ALU.mult)
                            else:
                                last = E.tensor_copy(
                                    out=h_b2[:, hq * TILE:(hq + 1) * TILE],
                                    in_=gw_scr2[mc])
                        return last
                    p.stage("dve", ev_h, reads=[gb("gwscr"), gb("cmbbc")],
                            writes=[gb("hB2")])
                w2b = gb("w2full")
                p.dma("sp", "w2ld",
                      [(w2full[:, kc * D:(kc + 1) * D], wbf_w2[e, kc, :, :])
                       for kc in range(8)],
                      writes=[w2b], reads=[gb("D_w2")])
                for ocg in range(4):
                    b0 = 0 if ocg % 2 == 0 else 4
                    bsrc = rb2_in if routed else sb2_in
                    boff = (e if routed else e - NE) * 2048 + ocg * 512
                    p.dma("sp", "rb2r",
                          [(rb2_row[:1, :512], bsrc[:1, boff:boff + 512])],
                          writes=[gb("rb2row")])

                    def w2m(E, ocg=ocg, b0=b0):
                        last = None
                        for kc in range(8):
                            for oo in range(4):
                                oc = ocg * 4 + oo
                                last = E.matmul(
                                    psum[b0 + oo][:, 0:512],
                                    w2full[:, kc * D + oc * 128:
                                           kc * D + oc * 128 + 128],
                                    h_b2[:, kc * TILE:(kc + 1) * TILE],
                                    start=(kc == 0), stop=False)
                        return last
                    p.stage("pe", w2m, reads=[w2b, gb("hB2")],
                            writes=[pbank[b0 + i] for i in range(4)])

                    def bm(E, b0=b0, routed=routed):
                        last = None
                        for oo in range(4):
                            lh = rb2_row[:1, oo * 128:(oo + 1) * 128]
                            rh = cmb_bc[0:1, :] if routed else ones_r[:1, :]
                            last = E.matmul(psum[b0 + oo][:, 0:512], lh, rh,
                                            start=False, stop=True)
                        return last
                    p.stage("pe", bm,
                            reads=[gb("rb2row"), gb("cmbbc"), gb("globals")],
                            writes=[pbank[b0 + i] for i in range(4)])
                    def accv(E, ocg=ocg, b0=b0):
                        last = None
                        for oo in range(4):
                            oc = ocg * 4 + oo
                            last = E.tensor_tensor(
                                out=ffn_acc[:, oc * TILE:(oc + 1) * TILE],
                                in0=ffn_acc[:, oc * TILE:(oc + 1) * TILE],
                                in1=psum[b0 + oo][:, 0:512], op=ALU.add)
                        return last
                    p.stage("dve", accv,
                            reads=[pbank[b0], pbank[b0 + 1], pbank[b0 + 2],
                                   pbank[b0 + 3], fb], writes=[fb])
            for tc in range(NCH):
                ssl = tc % 2
                slb = gb(f"sld{ssl}")
                p.dma("sp", f"sld{ssl}",
                      [(s_ld[ssl][:, kc * 128:(kc + 1) * 128],
                        st_s[t, kc, :, tc * 128:(tc + 1) * 128])
                       for kc in range(16)],
                      writes=[slb], reads=[gb("D_s")])
                xsl = tc % 2
                xlb = gb(f"xB2{xsl}")
                p.dma("sp", f"xld{xsl}",
                      [(x_ld[xsl], x_in[t * TILE + tc * 128:
                                        t * TILE + tc * 128 + 128, :])],
                      writes=[xlb])
                for g8 in range(2):
                    def strb(E, g8=g8, ssl=ssl):
                        last = None
                        for j in range(8):
                            kc = g8 * 8 + j
                            last = E.transpose(
                                psum[g8][:, :].bitcast(BF16)[
                                    :, j * 128:(j + 1) * 128],
                                s_ld[ssl][:, kc * 128:(kc + 1) * 128],
                                eye_bf[:, :])
                        return last
                    p.stage("pe", strb, reads=[slb, gb("globals")],
                            writes=[pbank[g8]])
                    p.stage("dve", lambda E, g8=g8: E.tensor_copy(
                        out=s_tm[:, g8 * 1024:(g8 + 1) * 1024],
                        in_=psum[g8][:, :].bitcast(BF16)[:, 0:1024]),
                        reads=[pbank[g8]], writes=[gb("stm")])
                for g4 in range(4):
                    def ftrb(E, g4=g4, tc=tc):
                        last = None
                        for j in range(4):
                            kc = g4 * 4 + j
                            last = E.transpose(
                                psum[2 + g4][:, j * 128:(j + 1) * 128],
                                ffn_acc[:, kc * TILE + tc * 128:
                                        kc * TILE + tc * 128 + 128],
                                eye_f[:, :])
                        return last
                    p.stage("pe", ftrb, reads=[fb, gb("globals")],
                            writes=[pbank[2 + g4]])
                    p.stage("act", lambda E, g4=g4: E.activation(
                        out=g_tm[:, g4 * 512:(g4 + 1) * 512],
                        in_=psum[2 + g4][:, 0:512], func=AF.Silu),
                        reads=[pbank[2 + g4]], writes=[gb("gtm")])
                p.stage("dve", lambda E: E.tensor_tensor(
                    out=s_tm, in0=s_tm, in1=g_tm, op=ALU.mult),
                    reads=[gb("stm"), gb("gtm")], writes=[gb("stm")])
                osl = tc % 2
                ob = gb(f"outB{osl}")
                p.stage("dve", lambda E, osl=osl, xsl=xsl: E.tensor_tensor(
                    out=out_b[osl], in0=s_tm, in1=x_ld[xsl], op=ALU.add),
                    reads=[gb("stm"), xlb], writes=[ob])
                p.dma("act", f"ost{osl}",
                      [(out_t[t * TILE + tc * 128:t * TILE + tc * 128 + 128,
                              :], out_b[osl])],
                      reads=[ob])

        ost = [p.dma_sems.get(f"ost{i}") for i in range(2)]

        @block.sync
        def _(E):
            for o in ost:
                if o is not None:
                    E.wait_ge(o[0], o[1])

        print(f"[build] stages={p.nstages}")
    return nc


def _reshape_inputs(inp):
    o = {}
    cw = np.asarray(inp["L0_attn_conv_w"], np.float32)
    o["conv_w_rs"] = np.ascontiguousarray(
        cw.reshape(34, 128, 4).transpose(1, 0, 2).reshape(128, 136))
    o["conv_b_rs"] = np.ascontiguousarray(
        np.asarray(inp["L0_attn_conv_b"], np.float32).reshape(34, 128).T)
    o["a_col"] = np.asarray(inp["L0_attn_A"], np.float32).reshape(64, 1)
    o["dtb_col"] = np.asarray(inp["L0_attn_dt_bias"],
                              np.float32).reshape(64, 1)
    o["d_bc"] = np.ascontiguousarray(np.broadcast_to(
        np.asarray(inp["L0_attn_D"], np.float32)[None, :], (128, 64)))
    o["norm_w_rs"] = np.ascontiguousarray(
        np.asarray(inp["L0_attn_norm_w"], np.float32).reshape(32, 128).T)
    o["wa_rs"] = np.ascontiguousarray(
        np.asarray(inp["L0_ffn_wa"], np.float32).reshape(16, 128, 16)
        .transpose(1, 0, 2).reshape(128, 256))
    for nm, key, ne in (("rgb_rs", "L0_ffn_rgb", NE),
                        ("rb1_rs", "L0_ffn_rb1", NE),
                        ("sgb_rs", "L0_ffn_sgb", NSH),
                        ("sb1_rs", "L0_ffn_sb1", NSH)):
        v = np.asarray(inp[key], np.float32).reshape(ne, 8, 128)
        o[nm] = np.ascontiguousarray(v.transpose(2, 0, 1).reshape(128, ne * 8))
    o["rb2_rs"] = np.ascontiguousarray(
        np.asarray(inp["L0_ffn_rb2"], np.float32).reshape(1, NE * 16 * 128))
    o["sb2_rs"] = np.ascontiguousarray(
        np.asarray(inp["L0_ffn_sb2"], np.float32).reshape(1, NSH * 16 * 128))
    o["eye_bf"] = np.eye(128, dtype=NPBF)
    o["eye_f"] = np.eye(128, dtype=np.float32)
    mu = np.zeros((128, 128), np.float32)
    mu[np.tril_indices(128, -1)] = MASK_NEG
    o["m_up"] = mu
    o["ones_in"] = np.ones((128, 8), np.float32)
    o["cst"] = np.ascontiguousarray(np.broadcast_to(np.array(
        [EPS, 1.0, -1.0, -0.5, 1.0 / D, 1.0 / DI, -10.0, 0.0],
        np.float32)[None, :], (128, 8)))
    o["onesr_in"] = np.ones((1, 512), np.float32)
    return o


def kernel(**inputs):
    inp = {k: np.asarray(v) for k, v in inputs.items()}
    x_full = np.ascontiguousarray(
        inp["L0_attn_hidden_states"].reshape(B * S, D).astype(np.float32,
                                                              copy=False))
    shared = _reshape_inputs(inp)
    shared["w_in"] = np.ascontiguousarray(
        inp["L0_attn_W_in"].astype(np.float32, copy=False))
    shared["w_out"] = np.ascontiguousarray(
        inp["L0_attn_W_out"].astype(np.float32, copy=False))
    for nm, key in (("rg", "L0_ffn_rg"), ("rw1", "L0_ffn_rw1"),
                    ("rw2", "L0_ffn_rw2"), ("sg", "L0_ffn_sg"),
                    ("sw1", "L0_ffn_sw1"), ("sw2", "L0_ffn_sw2")):
        shared[nm] = np.ascontiguousarray(inp[key].astype(np.float32,
                                                          copy=False))
    nc = build_bass(n_tiles=NT_BUILD)
    in_maps = []
    for c in range(NC):
        m = dict(shared)
        m["x"] = x_full[c * TOK:(c + 1) * TOK]
        if c % 4 == 0:
            m["xhalo"] = np.zeros((4, D), np.float32)
        else:
            m["xhalo"] = np.ascontiguousarray(np.concatenate(
                [x_full[c * TOK - 3:c * TOK], np.zeros((1, D), np.float32)],
                0))
        selT = np.zeros((8, 8), np.float32)
        maskT = np.full((64, 8), MASK_NEG, np.float32)
        base = (c // 4) * 4
        for j in range(base, c):
            maskT[:, j] = 0.0
            for i in range(j + 1, c):
                selT[i, j] = 1.0
        m["selT"] = selT
        m["maskT"] = maskT
        in_maps.append(m)
    res = run_bass_kernel_spmd(nc, in_maps, core_ids=list(range(NC)))
    out = np.concatenate([res.results[c]["out"] for c in range(NC)], axis=0)
    return out.reshape(B, S, D)
